# version 09
# baseline (speedup 1.0000x reference)
import math
from contextlib import ExitStack

import numpy as np

import concourse.bass as bass
import concourse.mybir as mybir
from concourse.bass_utils import run_bass_kernel_spmd

DBG = None
DBG2 = None
DBG3 = False
DBGT = None
DBG5 = None
DBG6 = False
DBG7 = False
DBG8 = False

F32 = mybir.dt.float32
BF16 = mybir.dt.bfloat16
FP8 = mybir.dt.float8e4
AF = mybir.ActivationFunctionType
ALU = mybir.AluOpType
AX = mybir.AxisListType

NCORES = 8
HID = 2048
NDC = HID // 128
LTOK = 1024
H = 8
DH = 128
CT = 64
NCHUNK = LTOK // CT
MEGA = 256
NMEGA = LTOK // MEGA
CPM = MEGA // CT
FTT = 256
NFTT = LTOK // FTT
EPS = 1e-5
NEG = -1e30
LNDH_HALF = 0.5 * math.log(DH)
KF = 8


class Sem:
    def __init__(self, ctx, nc, name):
        self.h = ctx.enter_context(nc.semaphore(name))
        self.n = 0

    def inc(self, instr, k=1):
        self.n += k
        instr.then_inc(self.h, k)
        return instr

    def wait(self, eng):
        eng.wait_ge(self.h, self.n)


class Arena:
    def __init__(self, t, tot):
        self.t = t
        self.tot = tot
        self.off = 0

    def alloc(self, width, parts=128, p0=0):
        o = self.off
        self.off += width
        assert self.off <= self.tot, f"arena overflow: {self.off} > {self.tot}"
        return self.t[p0:p0 + parts, o:o + width]

    def sub(self, width, parts, p0):
        return self.t[p0:p0 + parts, self.off - width:self.off]

    def mark(self):
        return self.off

    def reset(self, mark):
        self.off = mark


def build_nc():
    nc = bass.Bass("TRN2", target_bir_lowering=False, debug=False,
                   detect_race_conditions=False)

    def din(name, shape):
        return nc.dram_tensor(name, list(shape), F32, kind="ExternalInput").ap()

    xT = din("xT", (HID, LTOK + 3))
    wqT = din("wqT", (HID, 1024))
    wkT = din("wkT", (HID, 2048))
    wvT = din("wvT", (HID, 2048))
    wbaT = din("wbaT", (HID, 24))
    wgT = din("wgT", (HID, 1024))
    woT = din("woT", (1024, HID))
    wgate = din("wgate", (HID, 6144))
    wup = din("wup", (HID, 6144))
    wdown = din("wdown", (6144, HID))
    convq = din("convq", (1024, 4))
    convk = din("convk", (2048, 4))
    convv = din("convv", (2048, 4))
    alog = din("alog", (1, 8))
    dtb = din("dtb", (1, 8))
    onw = din("onw", (1, 128))
    gsc = din("gsc", (1, 6144))
    usc = din("usc", (1, 6144))
    dsc = din("dsc", (1, 2048))
    cvec = din("cvec", (1, 2048))
    wmodsl = din("wmodsl", (HID, 12288 // NCORES))
    hz = din("hz", (128, 3))
    rmask = din("rmask", (128, 8))
    rmaskinv = din("rmaskinv", (128, 8))

    out = nc.dram_tensor("out", [HID, LTOK], F32, kind="ExternalOutput").ap()
    if (DBG is not None or DBG2 is not None or DBGT is not None
            or DBG3 or DBG5 is not None or DBG6 or DBG7 or DBG8):
        dbg = nc.dram_tensor("dbg", [128, 6400], F32, kind="ExternalOutput").ap()

    modsh = nc.dram_tensor("modsh", [12, 128], F32).ap()
    modag = nc.dram_tensor("modag", [96, 128], F32).ap()
    pzin = nc.dram_tensor("pzin", [128, 2 * H * DH], F32).ap()
    pzag = nc.dram_tensor("pzag", [NCORES * 128, 2 * H * DH], F32).ap()
    _kind = dict(kind="ExternalOutput") if DBG3 else {}
    dOl = nc.dram_tensor("dOl", [NCHUNK * H * CT, DH], F32, **_kind).ap()
    dQc = nc.dram_tensor("dQc", [128, NCHUNK * H * CT], F32, **_kind).ap()
    dx1 = nc.dram_tensor("dx1", [HID, LTOK], F32, **_kind).ap()

    dgg = nc.dram_tensor("dgg", [KF * 128, LTOK], F32).ap()
    d8g = nc.dram_tensor("d8g", [HID, 6144], BF16).ap()
    d8u = nc.dram_tensor("d8u", [HID, 6144], BF16).ap()
    d8d = nc.dram_tensor("d8d", [6144, HID], BF16).ap()

    ctx = ExitStack()
    ART = 52600
    ar_t = ctx.enter_context(nc.sbuf_tensor("arena", [128, ART], F32))
    A = Arena(ar_t, ART)

    xres = A.alloc(NDC * (LTOK + 3))
    ident = A.alloc(128)
    onescol = A.alloc(1)
    maskN = A.alloc(128)
    maskNT = A.alloc(128)
    maskQT = A.alloc(64)
    modfm = A.alloc(96)
    onecal = A.alloc(16)
    onecbl = A.alloc(16)
    gam = A.alloc(16)
    gbm = A.alloc(16)
    dsgb = A.alloc(16)
    dscf = A.alloc(16)
    gscf = A.alloc(48)
    uscf = A.alloc(48)
    onwrep = A.alloc(128, parts=64)
    cq_w = A.alloc(8 * 4)
    ck_w = A.alloc(16 * 4)
    cv_w = A.alloc(16 * 4)
    hzt = A.alloc(3)
    rmt = A.alloc(8)
    rmit = A.alloc(8)
    tails = A.alloc(40 * 3)
    onesr = A.alloc(128, parts=1)
    dncol = A.alloc(2, parts=8)
    epscol = A.alloc(1)
    rdhcol = A.alloc(1)

    PERSIST = A.mark()
    srow = A.alloc(2048, parts=1)
    sil_c = A.alloc(16)
    gtmp = A.alloc(32)
    nega = A.alloc(8, parts=1)
    dtbr = A.alloc(8, parts=1)
    onwrow = A.alloc(128, parts=1)

    pb = [ctx.enter_context(nc.psum_tensor(f"pb{i}", [128, 512], F32))
          for i in range(8)]

    sD = Sem(ctx, nc, "sD")
    sW = Sem(ctx, nc, "sW")
    sT = Sem(ctx, nc, "sT")
    sV = Sem(ctx, nc, "sV")
    sS = Sem(ctx, nc, "sS")
    sG = Sem(ctx, nc, "sG")
    sCC = Sem(ctx, nc, "sCC")

    blk = ctx.enter_context(nc.Block())

    def T(f): blk.tensor(f)
    def V(f): blk.vector(f)
    def S(f): blk.scalar(f)
    def G(f): blk.gpsimd(f)
    def SY(f): blk.sync(f)

    def barrier():
        def _t(e): sT.inc(e.drain())
        T(_t)
        def _v(e): sV.inc(e.drain())
        V(_v)
        def _s(e): sS.inc(e.drain())
        S(_s)
        def _g(e): sG.inc(e.drain())
        G(_g)
        def _w(e):
            for s in (sT, sV, sS, sG, sD, sW, sCC):
                s.wait(e)
        for sec in (T, V, S, G, SY):
            sec(_w)

    def taildump(phase):
        if phase is None or DBGT != phase:
            return False
        def _td(e):
            for s in (sT, sV, sS, sG, sW):
                s.wait(e)
            sD.inc(e.dma_start(out=dbg[:, 0:120], in_=tails[:, :]), 16)
            sD.inc(e.dma_start(out=dbg[:, 128:384], in_=kpe[:, 0, :]), 16)
            sD.inc(e.dma_start(out=dbg[:, 384:643], in_=uscr[:, :]), 16)
            sD.wait(e)
        SY(_td)
        return True

    def _p0_dma(e):
        for i in range(NDC):
            sD.inc(e.dma_start(out=xres[:, i * (LTOK + 3):(i + 1) * (LTOK + 3)],
                               in_=xT[i * 128:(i + 1) * 128, :]), 16)
        sD.inc(e.dma_start(out=srow[0:1, 0:2048], in_=cvec[0:1, :]), 16)
        sD.inc(e.dma_start(out=nega, in_=alog[0:1, :]), 16)
        sD.inc(e.dma_start(out=dtbr, in_=dtb[0:1, :]), 16)
        sD.inc(e.dma_start(out=hzt[:, :], in_=hz[:, :]), 16)
        sD.inc(e.dma_start(out=rmt[:, :], in_=rmask[:, :]), 16)
        sD.inc(e.dma_start(out=rmit[:, :], in_=rmaskinv[:, :]), 16)
        sD.inc(e.dma_start(out=onwrow, in_=onw[0:1, :]), 16)
        for (dst, src, nf) in ((cq_w, convq, 8), (ck_w, convk, 16), (cv_w, convv, 16)):
            d = dst.rearrange("p (f s) -> p f s", s=4)
            for f in range(nf):
                sD.inc(e.dma_start(out=d[:, f, :],
                                   in_=src[f * 128:(f + 1) * 128, :]), 16)
    SY(_p0_dma)

    def _p0_masks(g):
        g.memset(ident[:, :], 0.0)
        g.memset(maskN[:, :], 0.0)
        g.memset(maskNT[:, :], 0.0)
        g.memset(maskQT[:, :], 0.0)
        g.memset(onescol[:, :], 1.0)
        g.memset(onesr, 1.0)
        g.memset(epscol[:, :], 1e-5)
        g.memset(rdhcol[:, :], 1.0 / 128.0)
        g.drain().then_inc(sG.h, 1)
        sG.n += 1
        g.wait_ge(sG.h, sG.n)
        g.affine_select(out=ident[:, :], in_=ident[:, :], compare_op=ALU.not_equal,
                        fill=1.0, base=0, pattern=[[-1, 128]], channel_multiplier=1)
        for (jr, jp, base) in ((0, 0, -1), (0, 1, -1), (1, 0, 0), (1, 1, -1)):
            g.affine_select(out=maskN[jr * 64:jr * 64 + 64, jp * 64:jp * 64 + 64],
                            in_=maskN[jr * 64:jr * 64 + 64, jp * 64:jp * 64 + 64],
                            compare_op=ALU.is_ge, fill=NEG, base=base,
                            pattern=[[-1, 64]], channel_multiplier=1)
        for (ja, jb, base) in ((0, 0, -1), (0, 1, 0), (1, 0, -1), (1, 1, -1)):
            g.affine_select(out=maskNT[ja * 64:ja * 64 + 64, jb * 64:jb * 64 + 64],
                            in_=maskNT[ja * 64:ja * 64 + 64, jb * 64:jb * 64 + 64],
                            compare_op=ALU.is_ge, fill=NEG, base=base,
                            pattern=[[1, 64]], channel_multiplier=-1)
        for js in (0, 1):
            g.affine_select(out=maskQT[js * 64:js * 64 + 64, :],
                            in_=maskQT[js * 64:js * 64 + 64, :],
                            compare_op=ALU.is_ge, fill=NEG, base=0,
                            pattern=[[1, 64]], channel_multiplier=-1)
        g.drain().then_inc(sG.h, 1)
        sG.n += 1
    G(_p0_masks)

    def _p1_silu(e):
        sD.wait(e)
        e.activation(srow[0:1, 0:2048], srow[0:1, 0:2048], AF.Silu)
        e.activation(nega, nega, AF.Exp)
        e.mul(nega, nega, -1.0)
        sS.inc(e.drain())
    S(_p1_silu)

    def _p1_cmm(e):
        sS.wait(e)
        sG.wait(e)
        for j in range(16):
            e.matmul(pb[0][:, j:j + 1], srow[0:1, j * 128:(j + 1) * 128],
                     onescol[0:1, 0:1], start=True, stop=True)
        e.matmul(pb[0][0:8, 16:17], dtbr, onescol[0:1, 0:1], start=True, stop=True)
        e.matmul(pb[0][0:8, 17:18], nega, onescol[0:1, 0:1], start=True, stop=True)
        e.matmul(pb[1][0:64, 0:128], onesr[0:1, 0:64], onwrow, start=True, stop=True)
        sT.inc(e.drain())
    T(_p1_cmm)

    def _p1_cev(e):
        sT.wait(e)
        e.tensor_copy(sil_c[:, :], pb[0][:, 0:16])
        e.tensor_copy(dncol[0:8, :], pb[0][0:8, 16:18])
        e.tensor_copy(onwrep[:, :], pb[1][0:64, 0:128])
        sV.inc(e.drain())
    V(_p1_cev)

    NMC = 12288 // NCORES // 128
    PH = A.mark()
    wm_slab = A.alloc(2 * NMC * 128)
    mod12 = A.alloc(NMC)
    mod12t = A.alloc(128, parts=12)
    mod96 = A.alloc(128, parts=96)

    def _p1_wdma(e):
        bT = sT.n
        for dc in range(NDC):
            half = (dc % 2) * NMC * 128
            e.wait_ge(sT.h, bT + max(dc - 1, 0))
            sW.inc(e.dma_start(out=wm_slab[:, half:half + NMC * 128],
                               in_=wmodsl[dc * 128:(dc + 1) * 128, :]), 16)
    G(_p1_wdma)

    def _p1_mm2(e):
        sV.wait(e)
        for dc in range(NDC):
            half = (dc % 2) * NMC * 128
            e.wait_ge(sW.h, sW.n - 16 * (NDC - 1 - dc))
            for fc in range(NMC):
                e.matmul(pb[2][:, fc:fc + 1],
                         wm_slab[:, half + fc * 128:half + (fc + 1) * 128],
                         sil_c[:, dc:dc + 1],
                         start=(dc == 0 and fc == 0),
                         stop=(dc == NDC - 1 and fc == NMC - 1))
            sT.inc(e.drain())
    T(_p1_mm2)

    def _p1_ev2(e):
        sT.wait(e)
        sV.inc(e.tensor_copy(mod12[:, :], pb[2][:, 0:NMC]))
    V(_p1_ev2)

    def _p1_tr2(e):
        sV.wait(e)
        e.transpose(pb[3][0:12, 0:128], mod12[:, :], ident[:, :])
        sT.inc(e.drain())
    T(_p1_tr2)

    def _p1_ev3(e):
        sT.wait(e)
        sV.inc(e.tensor_copy(mod12t[:, :], pb[3][0:12, 0:128]))
    V(_p1_ev3)

    def _p1_ship(e):
        sV.wait(e)
        sD.inc(e.dma_start(out=modsh[:, :], in_=mod12t[:, :]), 16)
        sD.wait(e)
    SY(_p1_ship)

    def _p1_ag(g):
        sD.wait(g)
        sCC.inc(g.collective_compute(
            "AllGather", ALU.bypass, replica_groups=[list(range(NCORES))],
            ins=[modsh[:, :].opt()], outs=[modag[:, :].opt()]))
    G(_p1_ag)

    def _p1_back(e):
        sCC.wait(e)
        sD.inc(e.dma_start(out=mod96[:, :], in_=modag[:, :]), 16)
    SY(_p1_back)

    def _p1_tr3(e):
        sD.wait(e)
        e.transpose(pb[3][:, 0:96], mod96[:, :], ident[0:96, 0:96])
        sT.inc(e.drain())
    T(_p1_tr3)

    def _p1_ev4(e):
        sT.wait(e)
        sV.inc(e.tensor_copy(modfm[:, :], pb[3][:, 0:96]))
    V(_p1_ev4)

    def _p1_mods(e):
        sV.wait(e)
        e.tensor_scalar_add(onecal[:, :], modfm[:, 16:32], 1.0)
        e.tensor_scalar_add(onecbl[:, :], modfm[:, 64:80], 1.0)
        sV.inc(e.drain())
    V(_p1_mods)

    def _p1_cl_s(e):
        sV.wait(e)
        e.activation(gtmp[:, 0:16], modfm[:, 32:48], AF.Abs)
        e.activation(gtmp[:, 16:32], modfm[:, 80:96], AF.Abs)
        e.sign(gam[:, :], modfm[:, 32:48])
        e.sign(gbm[:, :], modfm[:, 80:96])
        sS.inc(e.drain())
    S(_p1_cl_s)

    def _p1_cl_v(e):
        sS.wait(e)
        e.tensor_scalar_max(gtmp[:, 0:16], gtmp[:, 0:16], 0.25)
        e.tensor_scalar_max(gtmp[:, 16:32], gtmp[:, 16:32], 0.25)
        e.tensor_mul(gam[:, :], gam[:, :], gtmp[:, 0:16])
        e.tensor_mul(gbm[:, :], gbm[:, :], gtmp[:, 16:32])
        sV.inc(e.drain())
    V(_p1_cl_v)

    def p1_scales(src_ap, dst, n16):
        for part in range((n16 + 15) // 16):
            w16 = min(16, n16 - part * 16)
            def _dma(e, part=part, w16=w16, src_ap=src_ap):
                sV.wait(e)
                sT.wait(e)
                sS.wait(e)
                sD.inc(e.dma_start(
                    out=srow[0:1, 0:w16 * 128],
                    in_=src_ap[0:1, part * 2048:part * 2048 + w16 * 128]), 16)
            SY(_dma)
            def _mm(e, w16=w16):
                sD.wait(e)
                for j in range(w16):
                    e.matmul(pb[4][:, j:j + 1], srow[0:1, j * 128:(j + 1) * 128],
                             onescol[0:1, 0:1], start=True, stop=True)
                sT.inc(e.drain())
            T(_mm)
            def _ev(e, part=part, w16=w16, dst=dst):
                sT.wait(e)
                sV.inc(e.tensor_copy(dst[:, part * 16:part * 16 + w16],
                                     pb[4][:, 0:w16]))
            V(_ev)
    p1_scales(gsc, gscf, 48)
    p1_scales(usc, uscf, 48)
    p1_scales(dsc, dscf, 16)

    def _p1_dsgb(e):
        sV.wait(e)
        sV.inc(e.tensor_mul(dsgb[:, :], dscf[:, :], gbm[:, :]))
    V(_p1_dsgb)

    barrier()

    A.reset(PERSIST)
    P2M = A.mark()
    xt_t = A.alloc(NDC * (MEGA + 3))
    kproj = A.alloc(2 * KF * MEGA)
    vproj = A.alloc(2 * KF * MEGA)
    qproj = A.alloc(KF * MEGA)
    uscr = A.alloc(MEGA + 3)
    gbuf = A.alloc(KF * MEGA)
    wslab = A.alloc(2 * 2048)
    bu = A.alloc(MEGA, parts=16)
    au = A.alloc(MEGA, parts=8)
    bval = A.alloc(MEGA, parts=16)
    lnb = A.alloc(MEGA, parts=16)
    gG = A.alloc(MEGA, parts=8)
    gdup = A.alloc(MEGA, parts=16)
    lnn_k = A.alloc(MEGA, parts=16)
    lnnq = A.alloc(MEGA, parts=8)
    rowexp = A.alloc(MEGA, parts=16)
    ncolexp = A.alloc(MEGA, parts=16)
    erow_t = A.alloc(MEGA, parts=16)
    qrow_t = A.alloc(MEGA, parts=8)
    exprow = A.alloc(MEGA, parts=16)
    experow = A.alloc(MEGA, parts=16)
    expcol = A.alloc(MEGA, parts=16)
    expqrow = A.alloc(MEGA, parts=8)
    zrow = A.alloc(64, parts=8)
    lnrow = A.alloc(2 * (MEGA + 3), parts=1)
    stg = A.alloc(512, parts=1)
    r2a = A.alloc(1024, parts=2)
    r2b = A.alloc(1024, parts=2)
    r2q = A.alloc(512, parts=2)
    colT = A.alloc(32)
    qcols = A.alloc(8, parts=64)
    gcols = A.alloc(8)
    dtmp = A.alloc(128)
    dtmp2 = A.alloc(128)
    dqtmp = A.alloc(64)
    nA_ = A.alloc(128)
    nAT_ = A.alloc(128)
    nB_ = A.alloc(128)
    nBT_ = A.alloc(128)
    rhs = A.alloc(256)
    ke = A.alloc(128)
    aqk = A.alloc(64)
    qtsb = A.alloc(128, parts=64)
    qeff = A.alloc(128, parts=64)
    qefft = A.alloc(64)
    ptile = A.alloc(128)
    pttile = A.alloc(128)
    sloc = A.alloc(H * DH)
    ppre = A.alloc(H * DH)
    A.alloc(128)
    oltmp = A.sub(128, 64, 0)
    qct = A.alloc(64)
    xsq = A.alloc(MEGA + 3)
    kctile = A.alloc(128)
    vctile = A.alloc(128)
    assert A.off <= ART, A.off

    def _z(g):
        g.memset(zrow, 0.0)
        g.memset(r2a[0:2, :], 1.0)
        g.memset(r2b[0:2, :], 1.0)
        g.memset(r2q[0:2, :], 1.0)
        g.memset(sloc[:, :], 0.0)
        g.memset(ppre[:, :], 0.0)
        g.drain().then_inc(sG.h, 1)
        sG.n += 1
    G(_z)
    barrier()

    xre = xres.rearrange("p (d t) -> p d t", d=NDC)
    xte = xt_t.rearrange("p (d t) -> p d t", d=NDC)
    kpe = kproj.rearrange("p (f t) -> p f t", f=2 * KF)
    vpe = vproj.rearrange("p (f t) -> p f t", f=2 * KF)
    qpe = qproj.rearrange("p (f t) -> p f t", f=KF)

    for mg in range(NMEGA):
        t0 = mg * MEGA
        W = MEGA + 3

        for dc in range(NDC):
            def _sq(e, dc=dc, t0=t0, W=W):
                if dc == 0:
                    sV.wait(e)
                else:
                    sT.wait(e)
                e.activation(xsq[:, :], xre[:, dc, t0:t0 + W], AF.Square)
                sS.inc(e.drain())
            S(_sq)
            def _mm(e, dc=dc, t0=t0, W=W):
                sS.wait(e)
                e.matmul(pb[0][0:1, 0:W], onescol[:, 0:1], xre[:, dc, t0:t0 + W],
                         start=(dc == 0), stop=(dc == NDC - 1))
                e.matmul(pb[1][0:1, 0:W], onescol[:, 0:1], xsq[:, :],
                         start=(dc == 0), stop=(dc == NDC - 1))
                sT.inc(e.drain())
            T(_mm)

        def _ln_v(e, W=W):
            sT.wait(e)
            e.tensor_scalar_mul(lnrow[0:1, 0:W], pb[0][0:1, 0:W], 1.0 / HID)
            e.tensor_scalar_mul(lnrow[0:1, W:2 * W], pb[1][0:1, 0:W], 1.0 / HID)
            e.tensor_mul(xsq[0:1, 0:W], lnrow[0:1, 0:W], lnrow[0:1, 0:W])
            e.tensor_sub(lnrow[0:1, W:2 * W], lnrow[0:1, W:2 * W], xsq[0:1, 0:W])
            sV.inc(e.drain())
        V(_ln_v)
        def _ln_s(e, W=W):
            sV.wait(e)
            e.activation(lnrow[0:1, W:2 * W], lnrow[0:1, W:2 * W], AF.Sqrt,
                         bias=epscol[0:1, 0:1])
            sS.inc(e.drain())
        S(_ln_s)
        def _ln_v2(e, W=W):
            sS.wait(e)
            sV.inc(e.reciprocal(lnrow[0:1, W:2 * W], lnrow[0:1, W:2 * W]))
        V(_ln_v2)
        def _ln_rep(e, W=W):
            sV.wait(e)
            e.matmul(pb[2][:, 0:W], onesr[0:1, 0:128], lnrow[0:1, 0:W],
                     start=True, stop=True)
            e.matmul(pb[3][:, 0:W], onesr[0:1, 0:128], lnrow[0:1, W:2 * W],
                     start=True, stop=True)
            sT.inc(e.drain())
        T(_ln_rep)
        for dc in range(NDC):
            def _m(e, dc=dc, mg=mg, t0=t0, W=W):
                if dc == 0:
                    sT.wait(e)
                e.tensor_sub(xte[:, dc, :], xre[:, dc, t0:t0 + W], pb[2][:, 0:W])
                e.tensor_mul(xte[:, dc, :], xte[:, dc, :], pb[3][:, 0:W])
                e.tensor_scalar(xte[:, dc, :], xte[:, dc, :],
                                onecal[:, dc:dc + 1], modfm[:, dc:dc + 1],
                                ALU.mult, ALU.add)
                if mg == 0:
                    e.tensor_mul(xte[:, dc, 0:3], xte[:, dc, 0:3], hzt[:, :])
                if dc == NDC - 1:
                    sV.inc(e.drain())
            V(_m)

        def wstream(wt, width):
            def _wd(e, wt=wt, width=width):
                bT = sT.n
                for dc in range(NDC):
                    half = (dc % 2) * 2048
                    e.wait_ge(sT.h, bT + max(dc - 1, 0))
                    sW.inc(e.dma_start(out=wslab[:, half:half + width],
                                       in_=wt[dc * 128:(dc + 1) * 128, :]), 16)
            G(_wd)

        def proj_mm(nfc):
            def _mm(e, nfc=nfc):
                sV.wait(e)
                for dc in range(NDC):
                    half = (dc % 2) * 2048
                    e.wait_ge(sW.h, sW.n - 16 * (NDC - 1 - dc))
                    for fc in range(nfc):
                        e.matmul(pb[fc // 2][:, (fc % 2) * 256:(fc % 2) * 256 + MEGA],
                                 wslab[:, half + fc * 128:half + (fc + 1) * 128],
                                 xte[:, dc, 3:3 + MEGA],
                                 start=(dc == 0 and fc % 2 == 0),
                                 stop=(dc == NDC - 1 and fc % 2 == 1))
                    sT.inc(e.drain())
            T(_mm)

        def conv_silu(nfc, dst, conv_w, tail_off):
            for fc in range(nfc):
                def _v(e, fc=fc, dst=dst, conv_w=conv_w, tail_off=tail_off):
                    if fc == 0:
                        sT.wait(e)
                    tl = tails[:, (tail_off + fc) * 3:(tail_off + fc) * 3 + 3]
                    e.tensor_copy(uscr[:, 0:3], tl)
                    e.tensor_copy(uscr[:, 3:3 + MEGA],
                                  pb[fc // 2][:, (fc % 2) * 256:(fc % 2) * 256 + MEGA])
                    e.tensor_copy(tl, pb[fc // 2][:, (fc % 2) * 256 + MEGA - 3:
                                                  (fc % 2) * 256 + MEGA])
                    wc = conv_w.rearrange("p (f s) -> p f s", s=4)
                    e.tensor_scalar_mul(dst[:, fc, :], uscr[:, 3:3 + MEGA],
                                        wc[:, fc, 3:4])
                    for s in (2, 1, 0):
                        e.scalar_tensor_tensor(dst[:, fc, :], uscr[:, s:s + MEGA],
                                               wc[:, fc, s:s + 1], dst[:, fc, :],
                                               ALU.mult, ALU.add)
                    sV.inc(e.drain())
                V(_v)
                def _s(e, fc=fc, dst=dst):
                    sV.wait(e)
                    e.activation(dst[:, fc, :], dst[:, fc, :], AF.Silu)
                    sS.inc(e.drain())
                S(_s)

        def halo_proj(wt, nfc, tail_off):
            if mg != 0:
                return
            wstream(wt, nfc * 128)
            def _mm(e, nfc=nfc):
                sV.wait(e)
                for dc in range(NDC):
                    half = (dc % 2) * 2048
                    e.wait_ge(sW.h, sW.n - 16 * (NDC - 1 - dc))
                    for fc in range(nfc):
                        e.matmul(pb[6][:, fc * 3:fc * 3 + 3],
                                 wslab[:, half + fc * 128:half + (fc + 1) * 128],
                                 xte[:, dc, 0:3],
                                 start=(dc == 0 and fc == 0),
                                 stop=(dc == NDC - 1 and fc == nfc - 1))
                    sT.inc(e.drain())
            T(_mm)
            def _ev(e, nfc=nfc, tail_off=tail_off):
                sT.wait(e)
                e.tensor_copy(tails[:, tail_off * 3:(tail_off + nfc) * 3],
                              pb[6][:, 0:nfc * 3])
                sV.inc(e.drain())
            V(_ev)

        halo_proj(wkT, 16, 0)
        wstream(wkT, 2048)
        proj_mm(16)
        conv_silu(16, kpe, ck_w, 0)
        if taildump(1 if mg == 0 else None):
            ctx.close()
            return nc
        halo_proj(wvT, 16, 16)
        wstream(wvT, 2048)
        proj_mm(16)
        conv_silu(16, vpe, cv_w, 16)
        if taildump(2 if mg == 0 else None):
            ctx.close()
            return nc
        halo_proj(wqT, 8, 32)
        wstream(wqT, 1024)
        proj_mm(8)
        conv_silu(8, qpe, cq_w, 32)

        if taildump(3 if mg == 0 else None):
            ctx.close()
            return nc
        def _ba_wd(e):
            sT.wait(e)
            sW.inc(e.dma_start(
                out=wslab[:, 0:384].rearrange("p (d f) -> p d f", d=NDC),
                in_=wbaT[:, :].rearrange("(d p) f -> p d f", p=128)), 16)
        G(_ba_wd)
        def _ba_mm(e):
            sW.wait(e)
            sV.wait(e)
            for dc in range(NDC):
                e.matmul(pb[6][0:16, 0:MEGA], wslab[:, dc * 24:dc * 24 + 16],
                         xte[:, dc, 3:3 + MEGA],
                         start=(dc == 0), stop=(dc == NDC - 1))
                e.matmul(pb[7][0:8, 0:MEGA], wslab[:, dc * 24 + 16:dc * 24 + 24],
                         xte[:, dc, 3:3 + MEGA],
                         start=(dc == 0), stop=(dc == NDC - 1))
            sT.inc(e.drain())
        T(_ba_mm)
        def _ba_ev(e):
            sT.wait(e)
            e.tensor_copy(bu[0:16, :], pb[6][0:16, 0:MEGA])
            e.tensor_copy(au[0:8, :], pb[7][0:8, 0:MEGA])
            sV.inc(e.drain())
        V(_ba_ev)

        wstream(wgT, 1024)
        proj_mm(8)
        def _g_ev(e, t0=t0):
            sT.wait(e)
            sD.wait(e)
            gb = gbuf.rearrange("p (f t) -> p f t", f=KF)
            for fc in range(KF):
                e.tensor_copy(gb[:, fc, :],
                              pb[fc // 2][:, (fc % 2) * 256:(fc % 2) * 256 + MEGA])
            sV.inc(e.drain())
        V(_g_ev)
        def _g_st(e, t0=t0):
            sV.wait(e)
            for fc in range(KF):
                sD.inc(e.dma_start(
                    out=dgg[fc * 128:(fc + 1) * 128, t0:t0 + MEGA],
                    in_=gbuf.rearrange("p (f t) -> p f t", f=KF)[:, fc, :]), 16)
        SY(_g_st)

        def _b_s1(e):
            sV.wait(e)
            e.activation(bval[0:16, :], bu[0:16, :], AF.Sigmoid)
            e.activation(lnb[0:16, :], bu[0:16, :], AF.Abs)
            sS.inc(e.drain())
        S(_b_s1)
        def _b_s2(e):
            sS.wait(e)
            e.activation(lnb[0:16, :], lnb[0:16, :], AF.Exp, scale=-1.0)
            sS.inc(e.drain())
        S(_b_s2)
        def _b_s3(e):
            sS.wait(e)
            e.activation(lnb[0:16, :], lnb[0:16, :], AF.Ln, bias=1.0)
            sS.inc(e.drain())
        S(_b_s3)
        def _b_v(e):
            sS.wait(e)
            e.tensor_scalar_mul(bval[0:16, :], bval[0:16, :], 2.0)
            e.tensor_scalar(xsq[0:16, 0:MEGA], bu[0:16, :],
                            -1.0, 0.0, ALU.mult, ALU.max)
            e.tensor_add(lnb[0:16, :], lnb[0:16, :], xsq[0:16, 0:MEGA])
            e.tensor_scalar(lnb[0:16, :], lnb[0:16, :],
                            -1.0, math.log(2.0), ALU.mult, ALU.add)
            sV.inc(e.drain())
        V(_b_v)

        def _g_v1(e):
            sV.wait(e)
            e.tensor_scalar_add(au[0:8, :], au[0:8, :], dncol[0:8, 0:1])
            sV.inc(e.drain())
        V(_g_v1)
        def _g_s1(e):
            sV.wait(e)
            e.activation(xsq[0:8, 0:MEGA], au[0:8, :], AF.Abs)
            sS.inc(e.drain())
        S(_g_s1)
        def _g_s2(e):
            sS.wait(e)
            e.activation(xsq[0:8, 0:MEGA], xsq[0:8, 0:MEGA], AF.Exp, scale=-1.0)
            sS.inc(e.drain())
        S(_g_s2)
        def _g_s3(e):
            sS.wait(e)
            e.activation(xsq[0:8, 0:MEGA], xsq[0:8, 0:MEGA], AF.Ln, bias=1.0)
            sS.inc(e.drain())
        S(_g_s3)
        def _g_v2(e):
            sS.wait(e)
            e.tensor_scalar_max(au[0:8, :], au[0:8, :], 0.0)
            e.tensor_add(au[0:8, :], au[0:8, :], xsq[0:8, 0:MEGA])
            e.tensor_scalar_mul(au[0:8, :], au[0:8, :], dncol[0:8, 1:2])
            for cc in range(CPM):
                e.tensor_tensor_scan(gG[0:8, cc * CT:(cc + 1) * CT],
                                     au[0:8, cc * CT:(cc + 1) * CT],
                                     zrow[0:8, 0:CT], 0.0, ALU.add, ALU.add)
            sV.inc(e.drain())
        V(_g_v2)
        def _g_d(e):
            sV.wait(e)
            sD.inc(e.dma_start(out=gdup[0:8, :], in_=gG[0:8, :]), 16)
            sD.inc(e.dma_start(out=gdup[8:16, :], in_=gG[0:8, :]), 16)
        SY(_g_d)

        if taildump(4 if mg == 0 else None):
            ctx.close()
            return nc
        srcs = [(kpe, fc) for fc in range(16)] + [(qpe, fc) for fc in range(8)]
        for pr in range(12):
            for k2 in range(2):
                gfc = pr * 2 + k2
                src, fc = srcs[gfc]
                def _sq(e, src=src, fc=fc, gfc=gfc):
                    sD.wait(e)
                    if gfc == 0:
                        sV.wait(e)
                    else:
                        sT.wait(e)
                    e.activation(xsq[:, 0:MEGA], src[:, fc, :], AF.Square)
                    sS.inc(e.drain())
                S(_sq)
                def _mm(e, k2=k2):
                    sS.wait(e)
                    e.matmul(pb[6][0:1, k2 * 256:k2 * 256 + MEGA],
                             onescol[:, 0:1], xsq[:, 0:MEGA],
                             start=True, stop=True)
                    sT.inc(e.drain())
                T(_mm)
            def _ev(e):
                sT.wait(e)
                sV.inc(e.tensor_copy(stg[0:1, 0:512], pb[6][0:1, 0:512]))
            V(_ev)
            def _sc(e, pr=pr):
                sV.wait(e)
                if pr < 8:
                    d = lnn_k[2 * pr:2 * pr + 2, :]
                else:
                    d = lnnq[2 * (pr - 8):2 * (pr - 8) + 2, :]
                sD.inc(e.dma_start(
                    out=d, in_=stg[0:1, 0:512].rearrange("p (a t) -> p a t", a=2)), 16)
            SY(_sc)
        def _n_v(e):
            sD.wait(e)
            e.tensor_scalar_max(lnn_k[0:16, :], lnn_k[0:16, :], 1e-24)
            e.tensor_scalar_max(lnnq[0:8, :], lnnq[0:8, :], 1e-24)
            sV.inc(e.drain())
        V(_n_v)
        def _n_s(e):
            sV.wait(e)
            e.activation(lnn_k[0:16, :], lnn_k[0:16, :], AF.Ln)
            e.activation(lnnq[0:8, :], lnnq[0:8, :], AF.Ln)
            sS.inc(e.drain())
        S(_n_s)
        def _n_v2(e):
            sS.wait(e)
            e.tensor_scalar_mul(lnn_k[0:16, :], lnn_k[0:16, :], 0.5)
            e.tensor_scalar_mul(lnnq[0:8, :], lnnq[0:8, :], 0.5)
            sV.inc(e.drain())
        V(_n_v2)

        def _rows2(e):
            sV.wait(e)
            gsrc = gdup[0:16, :].rearrange("p (c l) -> p c l", c=CPM)
            for cc in range(CPM):
                e.tensor_copy(xsq[0:16, cc * CT:(cc + 1) * CT],
                              gsrc[:, cc, CT - 1:CT].broadcast_to([16, CT]))
            e.tensor_sub(erow_t[0:16, :], xsq[0:16, 0:MEGA], gdup[0:16, :])
            e.tensor_sub(erow_t[0:16, :], erow_t[0:16, :], lnn_k[0:16, :])
            e.scalar_tensor_tensor(qrow_t[0:8, :], gdup[0:8, :], 1.0,
                                   lnnq[0:8, :], ALU.mult, ALU.subtract)
            e.tensor_scalar_add(qrow_t[0:8, :], qrow_t[0:8, :], -LNDH_HALF)
            e.tensor_sub(rowexp[0:16, :], lnb[0:16, :], lnn_k[0:16, :])
            e.tensor_add(rowexp[0:16, :], rowexp[0:16, :], gdup[0:16, :])
            e.tensor_add(ncolexp[0:16, :], gdup[0:16, :], lnn_k[0:16, :])
            e.tensor_scalar_mul(ncolexp[0:16, :], ncolexp[0:16, :], -1.0)
            sV.inc(e.drain())
        V(_rows2)
        def _v1s(e):
            sV.wait(e)
            e.activation(exprow[0:16, :], rowexp[0:16, :], AF.Exp)
            e.activation(experow[0:16, :], erow_t[0:16, :], AF.Exp)
            e.activation(expcol[0:16, :], ncolexp[0:16, :], AF.Exp, scale=-1.0)
            e.activation(expqrow[0:8, :], qrow_t[0:8, :], AF.Exp)
            sS.inc(e.drain())
        S(_v1s)

        if taildump(5 if mg == 0 else None):
            ctx.close()
            return nc
        for cc in range(CPM):
            cg = mg * CPM + cc
            ts = t0 + cc * CT

            tl0 = cc * CT
            def _r2d(e, tl0=tl0):
                sV.wait(e)
                sS.wait(e)
                da = r2a[0:1, :].rearrange("p (h j l) -> p h j l", h=8, j=2)
                sD.inc(e.dma_start(out=da[:, :, 0, :], in_=rowexp[0:8, tl0:tl0 + CT]), 16)
                sD.inc(e.dma_start(out=da[:, :, 1, :], in_=rowexp[8:16, tl0:tl0 + CT]), 16)
                db = r2b[1:2, :].rearrange("p (h j l) -> p h j l", h=8, j=2)
                sD.inc(e.dma_start(out=db[:, :, 0, :], in_=ncolexp[0:8, tl0:tl0 + CT]), 16)
                sD.inc(e.dma_start(out=db[:, :, 1, :], in_=ncolexp[8:16, tl0:tl0 + CT]), 16)
                dq = r2q[0:1, :].rearrange("p (h l) -> p h l", h=8)
                sD.inc(e.dma_start(out=dq[:, :, :], in_=qrow_t[0:8, tl0:tl0 + CT]), 16)
            SY(_r2d)
            def _ctr(e, tl0=tl0):
                sV.wait(e)
                sS.wait(e)
                e.transpose(pb[7][0:64, 0:16], exprow[0:16, tl0:tl0 + CT],
                            ident[0:16, 0:16])
                e.transpose(pb[7][0:64, 16:32], bval[0:16, tl0:tl0 + CT],
                            ident[0:16, 0:16])
                e.transpose(pb[7][0:64, 32:48], experow[0:16, tl0:tl0 + CT],
                            ident[0:16, 0:16])
                e.transpose(pb[7][0:64, 48:64], expcol[0:16, tl0:tl0 + CT],
                            ident[0:16, 0:16])
                e.transpose(pb[7][0:64, 64:72], expqrow[0:8, tl0:tl0 + CT],
                            ident[0:8, 0:8])
                sT.inc(e.drain())
            T(_ctr)
            def _ctv(e):
                sT.wait(e)
                src = pb[7][0:64, 0:64].rearrange("p (q j h) -> p q j h", q=4, j=2)
                cl = colT.rearrange("p (q h) -> p q h", q=4)
                e.tensor_copy(cl[0:64, :, :], src[:, :, 0, :])
                e.tensor_copy(colT[64:128, :].rearrange("p (q h) -> p q h", q=4),
                              src[:, :, 1, :])
                e.tensor_copy(qcols[0:64, :], pb[7][0:64, 64:72])
                e.tensor_mul(gcols[:, :], colT[:, 16:24], colT[:, 24:32])
                sV.inc(e.drain())
            V(_ctv)

            for h in range(H):
                kap = kpe[:, h:h + 9:8, cc * CT:(cc + 1) * CT]
                vap = vpe[:, h:h + 9:8, cc * CT:(cc + 1) * CT]
                qap = qpe[:, h, cc * CT:(cc + 1) * CT]
                bk = pb[4 + (h % 2)]
                bt = pb[6]

                def _kvc(e, kap=kap, vap=vap):
                    sV.wait(e)
                    kcv = kctile.rearrange("p (j l) -> p j l", j=2)
                    vcv = vctile.rearrange("p (j l) -> p j l", j=2)
                    e.tensor_copy(kcv, kap)
                    e.tensor_copy(vcv, vap)
                    sV.inc(e.drain())
                V(_kvc)

                def _mm1(e, h=h, qap=qap, bk=bk):
                    sV.wait(e)
                    if h == 0:
                        sD.wait(e)
                    e.matmul(bk[:, 0:128], kctile[:, :], kctile[:, :],
                             start=True, stop=True)
                    e.matmul(bk[:, 128:192], kctile[:, :], qap,
                             start=True, stop=True)
                    e.matmul(bk[:, 192:320], r2a[0:2, h * 128:(h + 1) * 128],
                             r2b[0:2, h * 128:(h + 1) * 128], start=True, stop=True)
                    e.matmul(bk[:, 320:448], r2b[0:2, h * 128:(h + 1) * 128],
                             r2a[0:2, h * 128:(h + 1) * 128], start=True, stop=True)
                    e.matmul(bk[:, 448:512], r2b[0:2, h * 128:(h + 1) * 128],
                             r2q[0:2, h * 64:(h + 1) * 64], start=True, stop=True)
                    sT.inc(e.drain())
                T(_mm1)
                def _dv1(e, bk=bk):
                    sT.wait(e)
                    e.tensor_add(dtmp[:, :], bk[:, 192:320], maskN[:, :])
                    e.tensor_add(dtmp2[:, :], bk[:, 320:448], maskNT[:, :])
                    e.tensor_add(dqtmp[:, :], bk[:, 448:512], maskQT[:, :])
                    sV.inc(e.drain())
                V(_dv1)
                def _ds1(e):
                    sV.wait(e)
                    e.activation(dtmp[:, :], dtmp[:, :], AF.Exp)
                    e.activation(dtmp2[:, :], dtmp2[:, :], AF.Exp)
                    e.activation(dqtmp[:, :], dqtmp[:, :], AF.Exp)
                    sS.inc(e.drain())
                S(_ds1)
                def _dv2(e, bk=bk):
                    sS.wait(e)
                    e.tensor_mul(nA_[:, :], bk[:, 0:128], dtmp[:, :])
                    e.tensor_mul(nAT_[:, :], bk[:, 0:128], dtmp2[:, :])
                    e.tensor_mul(aqk[:, :], bk[:, 128:192], dqtmp[:, :])
                    sV.inc(e.drain())
                V(_dv2)
                if DBG == (mg, cc, h):
                    def _dbgv(e, bk=bk):
                        sV.wait(e)
                        e.tensor_copy(xsq[:, 0:128], bk[:, 0:128])
                        sV.inc(e.drain())
                    V(_dbgv)
                    def _dbgd(e):
                        sV.wait(e)
                        for i, (t, w) in enumerate(((nA_, 128), (nAT_, 128),
                                                    (dtmp, 128), (dtmp2, 128),
                                                    (kctile, 128), (colT, 32),
                                                    (ident, 128))):
                            sD.inc(e.dma_start(out=dbg[:, i * 128:i * 128 + w],
                                               in_=t[:, 0:w]), 16)
                        sD.inc(e.dma_start(out=dbg[:, 896:1024], in_=xsq[:, 0:128]), 16)
                        sD.inc(e.dma_start(out=dbg[0:2, 1024:2048], in_=r2a[0:2, :]), 16)
                        sD.inc(e.dma_start(out=dbg[0:2, 2048:3072], in_=r2b[0:2, :]), 16)
                        sD.inc(e.dma_start(out=dbg[0:16, 3072:3072 + 256],
                                           in_=rowexp[0:16, :]), 16)
                        sD.inc(e.dma_start(out=dbg[0:16, 3328:3328 + 256],
                                           in_=ncolexp[0:16, :]), 16)
                        sD.inc(e.dma_start(out=dbg[0:16, 3584:3584 + 256],
                                           in_=lnn_k[0:16, :]), 16)
                        sD.inc(e.dma_start(out=dbg[0:16, 3840:3840 + 256],
                                           in_=gdup[0:16, :]), 16)
                        sD.inc(e.dma_start(out=dbg[:, 4096:4192], in_=modfm[:, :]), 16)
                        sD.inc(e.dma_start(out=dbg[:, 4192:4451],
                                           in_=xte[:, 0, :]), 16)
                        sD.inc(e.dma_start(out=dbg[:, 4451:4710],
                                           in_=xte[:, 15, :]), 16)
                        sD.inc(e.dma_start(out=dbg[0:8, 4710:4966], in_=au[0:8, :]), 16)
                        sD.inc(e.dma_start(out=dbg[0:16, 4966:5222], in_=bu[0:16, :]), 16)
                        sD.inc(e.dma_start(out=dbg[0:8, 5222:5478], in_=gG[0:8, :]), 16)
                        sD.inc(e.dma_start(out=dbg[0:8, 5478:5480], in_=dncol[0:8, :]), 16)
                        sD.inc(e.dma_start(out=dbg[0:16, 5480:5736], in_=lnb[0:16, :]), 16)
                        sD.inc(e.dma_start(out=dbg[0:16, 5736:5992], in_=bval[0:16, :]), 16)
                        sD.inc(e.dma_start(out=dbg[:, 5992:6056],
                                           in_=kpe[:, 0, 0:64]), 16)
                        sD.inc(e.dma_start(out=dbg[:, 6056:6120],
                                           in_=kpe[:, 8, 0:64]), 16)
                        sD.inc(e.dma_start(out=dbg[:, 6120:6240],
                                           in_=tails[:, :]), 16)
                        sD.wait(e)
                    SY(_dbgd)
                    ctx.close()
                    return nc
                def _tr(e, qap=qap, bt=bt):
                    sV.wait(e)
                    e.transpose(bt[:, 0:128], kctile[:, :], ident[:, :])
                    e.transpose(bt[:, 128:256], vctile[:, :], ident[:, :])
                    e.transpose(bt[0:64, 256:384], qap, ident[:, :])
                    sT.inc(e.drain())
                T(_tr)
                def _rhsv(e, h=h, bt=bt):
                    sT.wait(e)
                    e.tensor_scalar_mul(rhs[:, 0:128], bt[:, 128:256],
                                        colT[:, 8 + h:9 + h])
                    e.tensor_scalar_mul(rhs[:, 128:256], bt[:, 0:128],
                                        colT[:, 0 + h:1 + h])
                    e.tensor_scalar_mul(ke[:, :], bt[:, 0:128],
                                        colT[:, 16 + h:17 + h])
                    e.tensor_copy(qtsb[0:64, :], bt[0:64, 256:384])
                    sV.inc(e.drain())
                V(_rhsv)
                cur = [nA_, nAT_]
                nxt = [nB_, nBT_]
                for j in range(7):
                    def _ap(e, cur=tuple(cur), j=j):
                        sV.wait(e)
                        e.matmul(pb[j % 2][:, 0:256], cur[1][:, :], rhs[:, :],
                                 start=True, stop=True)
                        sT.inc(e.drain())
                    T(_ap)
                    if j < 6:
                        def _sqm(e, cur=tuple(cur)):
                            e.matmul(pb[2][:, 0:128], cur[1][:, :], cur[0][:, :],
                                     start=True, stop=True)
                            e.matmul(pb[2][:, 128:256], cur[0][:, :], cur[1][:, :],
                                     start=True, stop=True)
                            sT.inc(e.drain())
                        T(_sqm)
                    def _sub(e, j=j, nxt=tuple(nxt)):
                        sT.wait(e)
                        e.tensor_sub(rhs[:, :], rhs[:, :], pb[j % 2][:, 0:256])
                        if j < 6:
                            e.tensor_copy(nxt[0][:, :], pb[2][:, 0:128])
                            e.tensor_copy(nxt[1][:, :], pb[2][:, 128:256])
                        sV.inc(e.drain())
                    V(_sub)
                    cur, nxt = nxt, cur
                def _pm(e, bt=bt):
                    sV.wait(e)
                    e.matmul(bt[:, 384:512], ke[:, :], rhs[:, 128:256],
                             start=True, stop=True)
                    e.matmul(pb[3][:, 0:128], rhs[:, 128:256], ke[:, :],
                             start=True, stop=True)
                    e.matmul(pb[3][0:64, 128:256], aqk[:, :], rhs[:, 128:256],
                             start=True, stop=True)
                    sT.inc(e.drain())
                T(_pm)
                def _pv(e, h=h, bt=bt):
                    sT.wait(e)
                    e.scalar_tensor_tensor(ptile[:, :], ident[:, :],
                                           gcols[:, h:h + 1], bt[:, 384:512],
                                           ALU.mult, ALU.subtract)
                    e.scalar_tensor_tensor(pttile[:, :], ident[:, :],
                                           gcols[:, h:h + 1], pb[3][:, 0:128],
                                           ALU.mult, ALU.subtract)
                    e.scalar_tensor_tensor(qeff[0:64, :], qtsb[0:64, :],
                                           qcols[0:64, h:h + 1],
                                           pb[3][0:64, 128:256],
                                           ALU.mult, ALU.subtract)
                    sV.inc(e.drain())
                V(_pv)
                def _tm2(e, bt=bt):
                    sV.wait(e)
                    e.transpose(bt[:, 256:320], qeff[0:64, :], ident[0:64, 0:64])
                    sT.inc(e.drain())
                T(_tm2)
                def _tv2(e, bt=bt):
                    sT.wait(e)
                    sV.inc(e.tensor_copy(qefft[:, :], bt[:, 256:320]))
                V(_tv2)
                def _tm3(e, h=h, cg=cg):
                    sV.wait(e)
                    e.matmul(pb[3][0:64, 192:320], aqk[:, :], rhs[:, 0:128],
                             start=True, stop=False)
                    e.matmul(pb[3][0:64, 192:320], qefft[:, :],
                             sloc[:, h * DH:(h + 1) * DH], start=False, stop=True)
                    if cg > 0:
                        e.matmul(pb[3][:, 320:384], ppre[:, h * DH:(h + 1) * DH],
                                 qefft[:, :], start=True, stop=True)
                    sT.inc(e.drain())
                T(_tm3)
                def _tv3(e, cg=cg):
                    sT.wait(e)
                    e.tensor_copy(oltmp[0:64, :], pb[3][0:64, 192:320])
                    if cg > 0:
                        e.tensor_copy(qct[:, :], pb[3][:, 320:384])
                    else:
                        e.tensor_copy(qct[:, :], qefft[:, :])
                    sV.inc(e.drain())
                V(_tv3)
                def _dig(e, h=h, cg=cg):
                    sV.wait(e)
                    bi = cg * H + h
                    sD.inc(e.dma_start(out=dOl[bi * CT:(bi + 1) * CT, :],
                                       in_=oltmp[0:64, :]), 16)
                    sD.inc(e.dma_start(out=dQc[:, bi * CT:(bi + 1) * CT],
                                       in_=qct[:, :]), 16)
                SY(_dig)
                def _tm4(e, h=h, cg=cg):
                    sV.wait(e)
                    e.matmul(pb[3][:, 384:512], pttile[:, :],
                             sloc[:, h * DH:(h + 1) * DH], start=True, stop=False)
                    e.matmul(pb[3][:, 384:512], ke[:, :], rhs[:, 0:128],
                             start=False, stop=True)
                    if cg > 0:
                        e.matmul(pb[2][:, 256:384], pttile[:, :],
                                 ppre[:, h * DH:(h + 1) * DH], start=True, stop=True)
                    sT.inc(e.drain())
                T(_tm4)
                def _tv4(e, h=h, cg=cg):
                    sT.wait(e)
                    sD.wait(e)
                    e.tensor_copy(sloc[:, h * DH:(h + 1) * DH], pb[3][:, 384:512])
                    if cg > 0:
                        e.tensor_copy(ppre[:, h * DH:(h + 1) * DH],
                                      pb[2][:, 256:384])
                    else:
                        e.tensor_copy(ppre[:, h * DH:(h + 1) * DH], ptile[:, :])
                    sV.inc(e.drain())
                V(_tv4)
                if DBG2 == (mg, cc, h):
                    def _dbg2(e):
                        sV.wait(e)
                        for i, (t, w) in enumerate(((rhs, 256), (ke, 128),
                                                   (aqk, 64), (qefft, 64),
                                                   (ptile, 128), (pttile, 128))):
                            sD.inc(e.dma_start(out=dbg[:, i * 256:i * 256 + w],
                                               in_=t[:, 0:w]), 16)
                        sD.inc(e.dma_start(out=dbg[:, 1536:2560], in_=sloc[:, :]), 16)
                        sD.inc(e.dma_start(out=dbg[:, 2560:3584], in_=ppre[:, :]), 16)
                        sD.inc(e.dma_start(out=dbg[0:64, 3584:3712],
                                           in_=oltmp[0:64, :]), 16)
                        sD.inc(e.dma_start(out=dbg[:, 3712:3776], in_=qct[:, :]), 16)
                        sD.inc(e.dma_start(out=dbg[0:64, 3776:3904],
                                           in_=qeff[0:64, :]), 16)
                        sD.inc(e.dma_start(out=dbg[0:64, 3904:4032],
                                           in_=qtsb[0:64, :]), 16)
                        sD.inc(e.dma_start(out=dbg[:, 4032:4040],
                                           in_=gcols[:, :]), 16)
                        sD.inc(e.dma_start(out=dbg[0:64, 4040:4048],
                                           in_=qcols[0:64, :]), 16)
                        sD.wait(e)
                    SY(_dbg2)
                    ctx.close()
                    return nc

    barrier()

    def _p3_out(e):
        sD.inc(e.dma_start(out=pzin[:, 0:H * DH], in_=ppre[:, :]), 16)
        sD.inc(e.dma_start(out=pzin[:, H * DH:2 * H * DH], in_=sloc[:, :]), 16)
        sD.wait(e)
    SY(_p3_out)
    def _p3_ag(g):
        sD.wait(g)
        sCC.inc(g.collective_compute(
            "AllGather", ALU.bypass, replica_groups=[list(range(NCORES))],
            ins=[pzin[:, :].opt()], outs=[pzag[:, :].opt()]))
    G(_p3_ag)

    A.reset(P2M)
    pzbuf = A.alloc(2 * H * DH)
    putile = A.alloc(128)
    putt = A.alloc(128)
    zutile = A.alloc(128)
    s_in = A.alloc(H * DH)
    def _p3_zero(g):
        sCC.wait(g)
        g.memset(s_in[:, :], 0.0)
        g.drain().then_inc(sG.h, 1)
        sG.n += 1
    G(_p3_zero)
    for j in range(NCORES - 1):
        def _ld(e, j=j):
            sCC.wait(e)
            sV.wait(e)
            sD.inc(e.dma_start(out=pzbuf[:, :],
                               in_=pzag[j * 128:(j + 1) * 128, :]), 16)
        SY(_ld)
        def _iminv(e, j=j):
            sG.wait(e)
            sV.inc(e.tensor_scalar_mul(zutile[:, :], ident[:, :], rmit[:, j:j + 1]))
        V(_iminv)
        iminv_done = sV.n
        for h in range(H):
            def _mask(e, j=j, h=h):
                sD.wait(e)
                sT.wait(e)
                e.scalar_tensor_tensor(putile[:, :], pzbuf[:, h * DH:(h + 1) * DH],
                                       rmt[:, j:j + 1], zutile[:, :],
                                       ALU.mult, ALU.add)
                sV.inc(e.drain())
            V(_mask)
            def _trp(e):
                sV.wait(e)
                e.transpose(pb[0][:, 0:128], putile[:, :], ident[:, :])
                sT.inc(e.drain())
            T(_trp)
            def _evp(e):
                sT.wait(e)
                sV.inc(e.tensor_copy(putt[:, :], pb[0][:, 0:128]))
            V(_evp)
            def _fold(e, h=h):
                sV.wait(e)
                e.matmul(pb[1][:, 0:128], putt[:, :], s_in[:, h * DH:(h + 1) * DH],
                         start=True, stop=True)
                sT.inc(e.drain())
            T(_fold)
            def _fv(e, j=j, h=h):
                sT.wait(e)
                e.scalar_tensor_tensor(s_in[:, h * DH:(h + 1) * DH],
                                       pzbuf[:, (H + h) * DH:(H + h + 1) * DH],
                                       rmt[:, j:j + 1], pb[1][:, 0:128],
                                       ALU.mult, ALU.add)
                sV.inc(e.drain())
            V(_fv)

    olbuf = A.alloc(H * DH, parts=64)
    ggt = A.alloc(H * DH, parts=64)
    ggld = A.alloc(KF * CT)
    qcbuf = A.alloc(CT)
    A.alloc(8)
    msum = A.sub(8, 64, 0)
    osq = A.alloc(H * DH, parts=64)
    ofeat = A.alloc(KF * LTOK)
    woslab = A.alloc(2 * 1024)
    assert A.off <= ART, A.off
    ofe = ofeat.rearrange("p (f t) -> p f t", f=KF)

    if DBG3:
        def _d3(e):
            sV.wait(e)
            sT.wait(e)
            sD.inc(e.dma_start(out=dbg[:, 0:1024], in_=s_in[:, :]), 16)
            sD.wait(e)
        SY(_d3)

    for cg in range(NCHUNK):
        ts = cg * CT
        def _old(e, cg=cg):
            sV.wait(e)
            sT.wait(e)
            for h in range(H):
                bi = cg * H + h
                sD.inc(e.dma_start(out=olbuf[0:64, h * DH:(h + 1) * DH],
                                   in_=dOl[bi * CT:(bi + 1) * CT, :]), 16)
        SY(_old)
        for h in range(H):
            def _qld(e, cg=cg, h=h):
                sT.wait(e)
                bi = cg * H + h
                sD.inc(e.dma_start(out=qcbuf[:, :],
                                   in_=dQc[:, bi * CT:(bi + 1) * CT]), 16)
            SY(_qld)
            def _om(e, h=h):
                sD.wait(e)
                b = pb[0] if h < 4 else pb[1]
                e.matmul(b[0:64, (h % 4) * 128:(h % 4) * 128 + 128], qcbuf[:, :],
                         s_in[:, h * DH:(h + 1) * DH], start=True, stop=True)
                sT.inc(e.drain())
            T(_om)
        def _oadd(e):
            sT.wait(e)
            sD.wait(e)
            e.tensor_add(olbuf[0:64, 0:512], olbuf[0:64, 0:512], pb[0][0:64, 0:512])
            e.tensor_add(olbuf[0:64, 512:1024], olbuf[0:64, 512:1024],
                         pb[1][0:64, 0:512])
            sV.inc(e.drain())
        V(_oadd)
        def _ggl(e, ts=ts):
            sT.wait(e)
            gl = ggld.rearrange("p (f t) -> p f t", f=KF)
            for fc in range(KF):
                sD.inc(e.dma_start(out=gl[:, fc, :],
                                   in_=dgg[fc * 128:(fc + 1) * 128, ts:ts + CT]), 16)
        SY(_ggl)
        for h in range(H):
            def _ggt(e, h=h, ts=ts):
                if h == 0:
                    sS.wait(e)
                    sD.wait(e)
                b = pb[2] if h < 4 else pb[3]
                gl = ggld.rearrange("p (f t) -> p f t", f=KF)
                e.transpose(b[0:64, (h % 4) * 128:(h % 4) * 128 + 128],
                            gl[:, h, :], ident[:, :])
                sT.inc(e.drain())
            T(_ggt)
        def _ggs(e):
            sT.wait(e)
            e.activation(ggt[0:64, 0:512], pb[2][0:64, 0:512], AF.Silu)
            e.activation(ggt[0:64, 512:1024], pb[3][0:64, 0:512], AF.Silu)
            sS.inc(e.drain())
        S(_ggs)
        def _rms1(e):
            sV.wait(e)
            sS.wait(e)
            h4 = H // 2
            for half in (0, 1):
                sl = slice(half * 512, half * 512 + 512)
                e.tensor_tensor(osq[0:64, sl].rearrange("p (h d) -> p h d", h=h4),
                                olbuf[0:64, sl].rearrange("p (h d) -> p h d", h=h4),
                                olbuf[0:64, sl].rearrange("p (h d) -> p h d", h=h4),
                                ALU.mult)
            for half in (0, 1):
                sl = slice(half * 512, half * 512 + 512)
                e.tensor_reduce(msum[0:64, half * h4:half * h4 + h4],
                                osq[0:64, sl].rearrange("p (h d) -> p h d", h=h4),
                                AX.X, ALU.add)
            sV.inc(e.drain())
        V(_rms1)
        def _rms2(e):
            sV.wait(e)
            e.activation(msum[0:64, :], msum[0:64, :], AF.Sqrt,
                         scale=rdhcol[0:64, 0:1], bias=epscol[0:64, 0:1])
            sS.inc(e.drain())
        S(_rms2)
        def _rms3(e):
            sS.wait(e)
            ol3 = olbuf[0:64, :].rearrange("p (h d) -> p h d", h=H)
            e.reciprocal(msum[0:64, :], msum[0:64, :])
            e.tensor_tensor(ol3, ol3,
                            onwrep[0:64, :].rearrange("p (o d) -> p o d", o=1)
                            .broadcast_to([64, H, DH]), ALU.mult)
            e.tensor_tensor(ol3, ol3,
                            msum[0:64, :].rearrange("p (h o) -> p h o", o=1)
                            .broadcast_to([64, H, DH]), ALU.mult)
            e.tensor_mul(olbuf[0:64, :], olbuf[0:64, :], ggt[0:64, :])
            sV.inc(e.drain())
        V(_rms3)
        if DBG5 == cg:
            def _d5(e, cg=cg):
                sV.wait(e)
                sS.wait(e)
                sD.inc(e.dma_start(out=dbg[0:64, 0:1024], in_=olbuf[0:64, :]), 16)
                sD.inc(e.dma_start(out=dbg[0:64, 1024:2048], in_=ggt[0:64, :]), 16)
                sD.inc(e.dma_start(out=dbg[0:64, 2048:2056], in_=msum[0:64, :]), 16)
                sD.inc(e.dma_start(out=dbg[0:64, 2056:3080], in_=osq[0:64, :]), 16)
                sD.inc(e.dma_start(out=dbg[0:64, 3080:3208],
                                   in_=onwrep[0:64, :]), 16)
                sD.inc(e.dma_start(out=dbg[:, 3208:3272], in_=ggld[:, 0:64]), 16)
                sD.wait(e)
            SY(_d5)
            ctx.close()
            return nc
        for h in range(H):
            def _oft(e, h=h):
                if h == 0:
                    sV.wait(e)
                e.transpose(pb[4][:, h * 64:(h + 1) * 64],
                            olbuf[0:64, h * DH:(h + 1) * DH], ident[0:64, 0:64])
                sT.inc(e.drain())
            T(_oft)
        def _ofv(e, ts=ts):
            sT.wait(e)
            for h in range(H):
                e.tensor_copy(ofe[:, h, ts:ts + CT], pb[4][:, h * 64:(h + 1) * 64])
            sV.inc(e.drain())
        V(_ofv)

    if DBG6:
        def _d6(e):
            sV.wait(e)
            sD.inc(e.dma_start(out=dbg[:, 0:4096], in_=ofeat[:, 0:4096]), 16)
            sD.inc(e.dma_start(out=dbg[:, 4096:4112], in_=gam[:, :]), 16)
            sD.inc(e.dma_start(out=dbg[:, 4112:4368],
                               in_=xres[:, 3:259]), 16)
            sD.wait(e)
        SY(_d6)
        ctx.close()
        return nc

    for dc in range(NDC):
        def _wod(e, dc=dc):
            half = (dc % 2) * 1024
            if dc >= 2:
                e.wait_ge(sT.h, sT.n - 2)
            else:
                sT.wait(e)
            sW.inc(e.dma_start(
                out=woslab[:, half:half + 1024].rearrange("p (f d) -> p f d", f=8),
                in_=woT[:, dc * 128:(dc + 1) * 128]
                .rearrange("(f p) d -> p f d", p=128)), 16)
        G(_wod)
        for tt in range(2):
            def _wom(e, dc=dc, tt=tt):
                if tt == 0:
                    sV.wait(e)
                half = (dc % 2) * 1024
                sW.wait(e)
                for fk in range(8):
                    e.matmul(pb[5 + tt][:, 0:512],
                             woslab[:, half + fk * 128:half + (fk + 1) * 128],
                             ofe[:, fk, tt * 512:(tt + 1) * 512],
                             start=(fk == 0), stop=(fk == 7))
                sT.inc(e.drain())
            T(_wom)
        def _wov(e, dc=dc):
            sT.wait(e)
            for tt in range(2):
                xv = xre[:, dc, 3 + tt * 512:3 + (tt + 1) * 512]
                e.scalar_tensor_tensor(xv, pb[5 + tt][:, 0:512],
                                       gam[:, dc:dc + 1], xv, ALU.mult, ALU.add)
            sV.inc(e.drain())
        V(_wov)
        def _wos(e, dc=dc):
            sV.wait(e)
            sD.inc(e.dma_start(out=dx1[dc * 128:(dc + 1) * 128, :],
                               in_=xre[:, dc, 3:3 + LTOK]), 16)
        SY(_wos)

    barrier()

    A.reset(P2M)
    x1s = A.alloc(FTT)
    x1s2 = A.alloc(FTT)
    h2s = A.alloc(FTT)
    h2b_r = A.alloc(NDC * FTT)
    fuse_r = A.alloc(48 * FTT // 2 * 2)
    ftmp = A.alloc(FTT)
    fslab = A.alloc(2 * 2048)
    b8_r = A.alloc(2 * 1024)
    A.alloc(LTOK)
    lnr2 = A.sub(LTOK, 1, 0)[:, 0:2 * FTT]
    xsq2 = A.alloc(FTT)
    assert A.off <= ART, A.off
    q8t = ctx.enter_context(nc.sbuf_tensor("q8t", [128, 2048], FP8))
    h2b = h2b_r.bitcast(BF16)
    h2hi = h2b[:, 0:NDC * FTT].rearrange("p (d t) -> p d t", d=NDC)
    h2lo = h2b[:, NDC * FTT:2 * NDC * FTT].rearrange("p (d t) -> p d t", d=NDC)
    fuseb = fuse_r.bitcast(BF16)
    FW = 48 * FTT
    fhi = fuseb[:, 0:FW].rearrange("p (f t) -> p f t", f=48)
    flo = fuseb[:, FW:2 * FW].rearrange("p (f t) -> p f t", f=48)
    b8 = b8_r.bitcast(BF16)

    for ft in range(NFTT):
        tt0 = ft * FTT
        for dc in range(NDC):
            def _xl(e, dc=dc, ft=ft):
                sV.wait(e)
                sS.wait(e)
                sD.inc(e.dma_start(out=x1s[:, :],
                                   in_=dx1[dc * 128:(dc + 1) * 128,
                                           tt0:tt0 + FTT]), 16)
            SY(_xl)
            def _sq(e, dc=dc):
                sD.wait(e)
                if dc > 0:
                    sT.wait(e)
                e.activation(xsq2[:, :], x1s[:, :], AF.Square)
                sS.inc(e.drain())
            S(_sq)
            def _mm(e, dc=dc):
                sS.wait(e)
                e.matmul(pb[0][0:1, 0:FTT], onescol[:, 0:1], x1s[:, :],
                         start=(dc == 0), stop=(dc == NDC - 1))
                e.matmul(pb[1][0:1, 0:FTT], onescol[:, 0:1], xsq2[:, :],
                         start=(dc == 0), stop=(dc == NDC - 1))
                sT.inc(e.drain())
            T(_mm)
            def _w8(e, dc=dc):
                sT.wait(e)
                sV.inc(e.drain())
            V(_w8)
        def _f_v(e):
            sT.wait(e)
            e.tensor_scalar_mul(lnr2[0:1, 0:FTT], pb[0][0:1, 0:FTT], 1.0 / HID)
            e.tensor_scalar_mul(lnr2[0:1, FTT:2 * FTT], pb[1][0:1, 0:FTT],
                                1.0 / HID)
            e.tensor_mul(xsq2[0:1, 0:FTT], lnr2[0:1, 0:FTT], lnr2[0:1, 0:FTT])
            e.tensor_sub(lnr2[0:1, FTT:2 * FTT], lnr2[0:1, FTT:2 * FTT],
                         xsq2[0:1, 0:FTT])
            sV.inc(e.drain())
        V(_f_v)
        def _f_s(e):
            sV.wait(e)
            e.activation(lnr2[0:1, FTT:2 * FTT], lnr2[0:1, FTT:2 * FTT],
                         AF.Sqrt, bias=epscol[0:1, 0:1])
            sS.inc(e.drain())
        S(_f_s)
        def _f_v2(e):
            sS.wait(e)
            sV.inc(e.reciprocal(lnr2[0:1, FTT:2 * FTT], lnr2[0:1, FTT:2 * FTT]))
        V(_f_v2)
        def _f_rep(e):
            sV.wait(e)
            e.matmul(pb[2][:, 0:FTT], onesr[0:1, 0:128], lnr2[0:1, 0:FTT],
                     start=True, stop=True)
            e.matmul(pb[3][:, 0:FTT], onesr[0:1, 0:128], lnr2[0:1, FTT:2 * FTT],
                     start=True, stop=True)
            sT.inc(e.drain())
        T(_f_rep)
        for dc in range(NDC):
            def _xl2(e, dc=dc, ft=ft):
                if dc == 0:
                    sT.wait(e)
                sV.wait(e)
                sD.inc(e.dma_start(out=x1s[:, :],
                                   in_=dx1[dc * 128:(dc + 1) * 128,
                                           tt0:tt0 + FTT]), 16)
            SY(_xl2)
            def _h2(e, dc=dc):
                sD.wait(e)
                e.tensor_sub(h2s[:, :], x1s[:, :], pb[2][:, 0:FTT])
                e.tensor_mul(h2s[:, :], h2s[:, :], pb[3][:, 0:FTT])
                e.tensor_scalar(h2s[:, :], h2s[:, :],
                                onecbl[:, dc:dc + 1], modfm[:, 48 + dc:49 + dc],
                                ALU.mult, ALU.add)
                e.tensor_copy(h2hi[:, dc, :], h2s[:, :])
                e.tensor_sub(h2lo[:, dc, :], h2s[:, :], h2hi[:, dc, :])
                sV.inc(e.drain())
            V(_h2)

        def emit_gmm(dc, fb, ft):
            def _gmm(e, dc=dc, fb=fb, ft=ft):
                if ft == 0:
                    sS.wait(e)
                else:
                    sW.wait(e)
                if dc == 0:
                    sV.wait(e)
                bh = (dc % 2) * 2048
                for ps, h2x in ((0, h2hi), (1, h2lo)):
                    for fc in range(4):
                        e.matmul(pb[fc][:, 0:FTT],
                                 b8[:, bh + fc * 128:bh + (fc + 1) * 128],
                                 h2x[:, dc, :],
                                 start=(dc == 0 and ps == 0),
                                 stop=(dc == NDC - 1 and ps == 1))
                        e.matmul(pb[4 + fc][:, 0:FTT],
                                 b8[:, bh + 512 + fc * 128:
                                    bh + 512 + (fc + 1) * 128],
                                 h2x[:, dc, :],
                                 start=(dc == 0 and ps == 0),
                                 stop=(dc == NDC - 1 and ps == 1))
                sT.inc(e.drain())
            T(_gmm)

        for fb in range(12):
            if ft == 0:
                for dc in range(NDC):
                    def _wd1(e, dc=dc, fb=fb):
                        half = (dc % 2) * 1024
                        if dc >= 2:
                            e.wait_ge(sT.h, sT.n - 1)
                        else:
                            sT.wait(e)
                        for wi, wsrc in ((0, wgate), (1, wup)):
                            sW.inc(e.dma_start(
                                out=fslab[:, wi * 512 + half:wi * 512 + half + 512],
                                in_=wsrc[dc * 128:(dc + 1) * 128,
                                         fb * 512:(fb + 1) * 512]), 16)
                    G(_wd1)
                    def _cv(e, dc=dc, fb=fb):
                        sS.wait(e)
                        half = (dc % 2) * 1024
                        sW.wait(e)
                        e.tensor_copy(q8t[:, 0:512], fslab[:, half:half + 512])
                        e.tensor_copy(q8t[:, 512:1024],
                                      fslab[:, 512 + half:1024 + half])
                        sV.inc(e.drain())
                    V(_cv)
                    def _cs(e, dc=dc):
                        sV.wait(e)
                        if dc >= 2:
                            e.wait_ge(sT.h, sT.n - 1)
                            e.wait_ge(sD.h, sD.n - 32)
                        else:
                            sD.wait(e)
                            sT.wait(e)
                        bh = (dc % 2) * 2048
                        e.copy(b8[:, bh:bh + 1024], q8t[:, 0:1024])
                        sS.inc(e.drain())
                    S(_cs)
                    def _w8(e, dc=dc, fb=fb):
                        sS.wait(e)
                        bh = (dc % 2) * 2048
                        sD.inc(e.dma_start(
                            out=d8g[dc * 128:(dc + 1) * 128, fb * 512:(fb + 1) * 512],
                            in_=b8[:, bh:bh + 512]), 16)
                        sD.inc(e.dma_start(
                            out=d8u[dc * 128:(dc + 1) * 128, fb * 512:(fb + 1) * 512],
                            in_=b8[:, bh + 512:bh + 1024]), 16)
                    SY(_w8)
                    emit_gmm(dc, fb, ft)
            else:
                for dc in range(NDC):
                    def _wd8(e, dc=dc, fb=fb):
                        bh = (dc % 2) * 2048
                        if dc >= 2:
                            e.wait_ge(sT.h, sT.n - 1)
                        else:
                            sT.wait(e)
                        sW.inc(e.dma_start(
                            out=b8[:, bh:bh + 512],
                            in_=d8g[dc * 128:(dc + 1) * 128,
                                    fb * 512:(fb + 1) * 512]), 16)
                        sW.inc(e.dma_start(
                            out=b8[:, bh + 512:bh + 1024],
                            in_=d8u[dc * 128:(dc + 1) * 128,
                                    fb * 512:(fb + 1) * 512]), 16)
                    G(_wd8)
                    emit_gmm(dc, fb, ft)
            if False:
                def _gmm(e, dc=dc, fb=fb, ft=ft):
                    if ft == 0:
                        sS.wait(e)
                    else:
                        sW.wait(e)
                    if dc == 0:
                        sV.wait(e)
                    bh = (dc % 2) * 2048
                    for ps, h2x in ((0, h2hi), (1, h2lo)):
                        for fc in range(4):
                            e.matmul(pb[fc][:, 0:FTT],
                                     b8[:, bh + fc * 128:bh + (fc + 1) * 128],
                                     h2x[:, dc, :],
                                     start=(dc == 0 and ps == 0),
                                     stop=(dc == NDC - 1 and ps == 1))
                            e.matmul(pb[4 + fc][:, 0:FTT],
                                     b8[:, bh + 512 + fc * 128:
                                        bh + 512 + (fc + 1) * 128],
                                     h2x[:, dc, :],
                                     start=(dc == 0 and ps == 0),
                                     stop=(dc == NDC - 1 and ps == 1))
                    sT.inc(e.drain())
                T(_gmm)
            if DBG8 and ft == 0 and fb == 0:
                def _d8(e):
                    sT.wait(e)
                    e.tensor_copy(xsq[:, 0:FTT] if False else dbg[:, 0:0],
                                  dbg[:, 0:0]) if False else None
                    sV.inc(e.drain())
                V(_d8)
                def _d8v(e):
                    sT.wait(e)
                    e.tensor_copy(x1s[:, :], pb[0][:, 0:FTT])
                    e.tensor_copy(x1s2[:, :], pb[4][:, 0:FTT])
                    sV.inc(e.drain())
                V(_d8v)
                def _d8d(e):
                    sV.wait(e)
                    sD.inc(e.dma_start(out=dbg[:, 0:256], in_=x1s[:, :]), 16)
                    sD.inc(e.dma_start(out=dbg[:, 256:512], in_=x1s2[:, :]), 16)
                    sD.inc(e.dma_start(out=dbg[:, 512:1536], in_=h2b_r[:, 0:1024]), 16)
                    sD.inc(e.dma_start(out=dbg[:, 1536:2048], in_=b8_r[:, 0:512]), 16)
                    sD.wait(e)
                SY(_d8d)
                ctx.close()
                return nc
            for fc in range(4):
                def _ge(e, fb=fb, fc=fc):
                    sT.wait(e)
                    sV.wait(e)
                    gcol = fb * 4 + fc
                    e.activation(xsq2[:, 0:FTT], pb[fc][:, 0:FTT], AF.Silu,
                                 scale=gscf[:, gcol:gcol + 1])
                    sS.inc(e.drain())
                S(_ge)
                def _ue(e, fb=fb, fc=fc):
                    sS.wait(e)
                    gcol = fb * 4 + fc
                    e.scalar_tensor_tensor(ftmp[:, :], pb[4 + fc][:, 0:FTT],
                                           uscf[:, gcol:gcol + 1], xsq2[:, 0:FTT],
                                           ALU.mult, ALU.mult)
                    e.tensor_copy(fhi[:, gcol, :], ftmp[:, :])
                    e.tensor_sub(flo[:, gcol, :], ftmp[:, :], fhi[:, gcol, :])
                    sV.inc(e.drain())
                V(_ue)

        if DBG7 and ft == 0:
            def _d7(e):
                sV.wait(e)
                sS.wait(e)
                sD.inc(e.dma_start(out=dbg[:, 0:1024], in_=h2b_r[:, 0:1024]), 16)
                sD.inc(e.dma_start(out=dbg[:, 1024:2048], in_=fuse_r[:, 0:1024]), 16)
                sD.inc(e.dma_start(out=dbg[:, 2048:3072],
                                   in_=fuse_r[:, FW // 2:FW // 2 + 1024]), 16)
                sD.inc(e.dma_start(out=dbg[:, 3072:3584], in_=b8_r[:, 0:512]), 16)
                sD.inc(e.dma_start(out=dbg[:, 4096:4352], in_=xsq2[:, 0:FTT]), 16)
                sD.inc(e.dma_start(out=dbg[:, 4352:5376],
                                   in_=h2b_r[:, 2048:3072]), 16)
                sD.wait(e)
            SY(_d7)
            ctx.close()
            return nc
        for kc in range(48):
            if ft == 0:
                def _dwd(e, kc=kc):
                    half = (kc % 2) * 2048
                    if kc >= 2:
                        e.wait_ge(sT.h, sT.n - 1)
                    else:
                        sT.wait(e)
                    sW.inc(e.dma_start(out=fslab[:, half:half + 2048],
                                       in_=wdown[kc * 128:(kc + 1) * 128, :]), 16)
                G(_dwd)
                def _dcv(e, kc=kc):
                    sS.wait(e)
                    sW.wait(e)
                    half = (kc % 2) * 2048
                    e.tensor_copy(q8t[:, 0:2048], fslab[:, half:half + 2048])
                    sV.inc(e.drain())
                V(_dcv)
                def _dcs(e, kc=kc):
                    sV.wait(e)
                    if kc >= 2:
                        e.wait_ge(sT.h, sT.n - 1)
                        e.wait_ge(sD.h, sD.n - 16)
                    else:
                        sD.wait(e)
                        sT.wait(e)
                    bh = (kc % 2) * 2048
                    e.copy(b8[:, bh:bh + 2048], q8t[:, 0:2048])
                    sS.inc(e.drain())
                S(_dcs)
                def _dw8(e, kc=kc):
                    sS.wait(e)
                    bh = (kc % 2) * 2048
                    sD.inc(e.dma_start(out=d8d[kc * 128:(kc + 1) * 128, :],
                                       in_=b8[:, bh:bh + 2048]), 16)
                SY(_dw8)
            else:
                def _dwd8(e, kc=kc):
                    bh = (kc % 2) * 2048
                    if kc >= 2:
                        e.wait_ge(sT.h, sT.n - 1)
                    else:
                        sT.wait(e)
                    sW.inc(e.dma_start(out=b8[:, bh:bh + 2048],
                                       in_=d8d[kc * 128:(kc + 1) * 128, :]), 16)
                G(_dwd8)
            def _dmm(e, kc=kc, ft=ft):
                if ft == 0:
                    sS.wait(e)
                else:
                    sW.wait(e)
                if kc == 0:
                    sV.wait(e)
                bh = (kc % 2) * 2048
                for ps, fx in ((0, fhi), (1, flo)):
                    for dco in range(NDC):
                        e.matmul(pb[dco // 2][:, (dco % 2) * 256:(dco % 2) * 256 + FTT],
                                 b8[:, bh + dco * 128:bh + (dco + 1) * 128],
                                 fx[:, kc, :],
                                 start=(kc == 0 and dco % 2 == 0 and ps == 0),
                                 stop=(kc == 47 and dco % 2 == 1 and ps == 1))
                sT.inc(e.drain())
            T(_dmm)
        for dco in range(NDC):
            def _xl3(e, dco=dco, ft=ft):
                if dco == 0:
                    sT.wait(e)
                sV.wait(e)
                sD.wait(e)
                sD.inc(e.dma_start(out=x1s2[:, :],
                                   in_=dx1[dco * 128:(dco + 1) * 128,
                                           tt0:tt0 + FTT]), 16)
            SY(_xl3)
            def _dev(e, dco=dco):
                sD.wait(e)
                e.scalar_tensor_tensor(
                    x1s2[:, :],
                    pb[dco // 2][:, (dco % 2) * 256:(dco % 2) * 256 + FTT],
                    dsgb[:, dco:dco + 1], x1s2[:, :], ALU.mult, ALU.add)
                sV.inc(e.drain())
            V(_dev)
            def _ost(e, dco=dco, ft=ft):
                sV.wait(e)
                sD.inc(e.dma_start(out=out[dco * 128:(dco + 1) * 128,
                                           tt0:tt0 + FTT],
                                   in_=x1s2[:, :]), 16)
            SY(_ost)

    def _fin(e):
        sD.wait(e)
        sW.wait(e)
    SY(_fin)

    ctx.close()
    return nc


_NC_CACHE = None


def kernel(**inputs):
    global _NC_CACHE
    x = np.asarray(inputs["L0_attn_x"], np.float32)[0]
    Wq = np.asarray(inputs["L0_attn_Wq"], np.float32)
    Wk = np.asarray(inputs["L0_attn_Wk"], np.float32)
    Wv = np.asarray(inputs["L0_attn_Wv"], np.float32)
    Wb = np.asarray(inputs["L0_attn_Wb"], np.float32)
    Wa = np.asarray(inputs["L0_attn_Wa"], np.float32)
    Wg = np.asarray(inputs["L0_attn_Wg"], np.float32)
    Wo = np.asarray(inputs["L0_attn_Wo"], np.float32)

    wqT = np.ascontiguousarray(Wq.T)
    wkT = np.ascontiguousarray(Wk.T)
    wvT = np.ascontiguousarray(Wv.T)
    wbaT = np.ascontiguousarray(np.concatenate([Wb, Wa], 0).T)
    wgT = np.ascontiguousarray(Wg.T)
    woT = np.ascontiguousarray(Wo.T)
    wgate = np.ascontiguousarray(np.asarray(inputs["L0_ffn_gate_w"], np.float32))
    wup = np.ascontiguousarray(np.asarray(inputs["L0_ffn_up_w"], np.float32))
    wdown = np.ascontiguousarray(np.asarray(inputs["L0_ffn_down_w"], np.float32))
    wmod = np.asarray(inputs["L0_wmod"], np.float32)

    if _NC_CACHE is None:
        _NC_CACHE = build_nc()
    nc = _NC_CACHE

    in_maps = []
    for core in range(NCORES):
        lo = core * LTOK
        if core == 0:
            halo = np.zeros((3, HID), np.float32)
        else:
            halo = x[lo - 3:lo]
        xs = np.concatenate([halo, x[lo:lo + LTOK]], 0)
        m = {
            "xT": np.ascontiguousarray(xs.T),
            "wqT": wqT, "wkT": wkT, "wvT": wvT, "wbaT": wbaT, "wgT": wgT,
            "woT": woT, "wgate": wgate, "wup": wup, "wdown": wdown,
            "convq": np.ascontiguousarray(np.asarray(inputs["L0_attn_conv_q"], np.float32)),
            "convk": np.ascontiguousarray(np.asarray(inputs["L0_attn_conv_k"], np.float32)),
            "convv": np.ascontiguousarray(np.asarray(inputs["L0_attn_conv_v"], np.float32)),
            "alog": np.asarray(inputs["L0_attn_A_log"], np.float32).reshape(1, 8),
            "dtb": np.asarray(inputs["L0_attn_dt_bias"], np.float32).reshape(1, 8),
            "onw": np.asarray(inputs["L0_attn_o_norm_w"], np.float32).reshape(1, 128),
            "gsc": np.asarray(inputs["L0_ffn_gate_scale"], np.float32).reshape(1, -1),
            "usc": np.asarray(inputs["L0_ffn_up_scale"], np.float32).reshape(1, -1),
            "dsc": np.asarray(inputs["L0_ffn_down_scale"], np.float32).reshape(1, -1),
            "cvec": np.asarray(inputs["L0_c"], np.float32).reshape(1, -1),
            "wmodsl": np.ascontiguousarray(
                wmod[:, core * (12288 // NCORES):(core + 1) * (12288 // NCORES)]),
            "hz": np.full((128, 3), 0.0 if core == 0 else 1.0, np.float32),
            "rmask": np.ascontiguousarray(
                np.tile((np.arange(8) < core).astype(np.float32), (128, 1))),
            "rmaskinv": np.ascontiguousarray(
                np.tile((np.arange(8) >= core).astype(np.float32), (128, 1))),
        }
        in_maps.append(m)

    res = run_bass_kernel_spmd(nc, in_maps, core_ids=list(range(NCORES)))
    parts = [np.asarray(r["out"]).T for r in res.results]
    return np.concatenate(parts, 0)[None]


# version 10 (final)
# speedup vs baseline: 1.0300x; 1.0300x over previous
import math
from contextlib import ExitStack

import numpy as np

import concourse.bass as bass
import concourse.mybir as mybir
from concourse.bass_utils import run_bass_kernel_spmd

DBG = None
DBG2 = None
DBG3 = False
DBGT = None
DBG5 = None
DBG6 = False
DBG7 = False
DBG8 = False

F32 = mybir.dt.float32
BF16 = mybir.dt.bfloat16
FP8 = mybir.dt.float8e4
AF = mybir.ActivationFunctionType
ALU = mybir.AluOpType
AX = mybir.AxisListType

NCORES = 8
HID = 2048
NDC = HID // 128
LTOK = 1024
H = 8
DH = 128
CT = 64
NCHUNK = LTOK // CT
MEGA = 256
NMEGA = LTOK // MEGA
CPM = MEGA // CT
FTT = 256
NFTT = LTOK // FTT
EPS = 1e-5
NEG = -1e30
LNDH_HALF = 0.5 * math.log(DH)
KF = 8


class Sem:
    def __init__(self, ctx, nc, name):
        self.h = ctx.enter_context(nc.semaphore(name))
        self.n = 0

    def inc(self, instr, k=1):
        self.n += k
        instr.then_inc(self.h, k)
        return instr

    def wait(self, eng):
        eng.wait_ge(self.h, self.n)


class Arena:
    def __init__(self, t, tot):
        self.t = t
        self.tot = tot
        self.off = 0

    def alloc(self, width, parts=128, p0=0):
        o = self.off
        self.off += width
        assert self.off <= self.tot, f"arena overflow: {self.off} > {self.tot}"
        return self.t[p0:p0 + parts, o:o + width]

    def sub(self, width, parts, p0):
        return self.t[p0:p0 + parts, self.off - width:self.off]

    def mark(self):
        return self.off

    def reset(self, mark):
        self.off = mark


def build_nc():
    nc = bass.Bass("TRN2", target_bir_lowering=False, debug=False,
                   detect_race_conditions=False)

    def din(name, shape):
        return nc.dram_tensor(name, list(shape), F32, kind="ExternalInput").ap()

    xT = din("xT", (HID, LTOK + 3))
    wqT = din("wqT", (HID, 1024))
    wkT = din("wkT", (HID, 2048))
    wvT = din("wvT", (HID, 2048))
    wbaT = din("wbaT", (HID, 24))
    wgT = din("wgT", (HID, 1024))
    woT = din("woT", (1024, HID))
    wgate = din("wgate", (HID, 6144))
    wup = din("wup", (HID, 6144))
    wdown = din("wdown", (6144, HID))
    convq = din("convq", (1024, 4))
    convk = din("convk", (2048, 4))
    convv = din("convv", (2048, 4))
    alog = din("alog", (1, 8))
    dtb = din("dtb", (1, 8))
    onw = din("onw", (1, 128))
    gsc = din("gsc", (1, 6144))
    usc = din("usc", (1, 6144))
    dsc = din("dsc", (1, 2048))
    cvec = din("cvec", (1, 2048))
    wmodsl = din("wmodsl", (HID, 12288 // NCORES))
    hz = din("hz", (128, 3))
    rmask = din("rmask", (128, 8))
    rmaskinv = din("rmaskinv", (128, 8))

    out = nc.dram_tensor("out", [HID, LTOK], F32, kind="ExternalOutput").ap()
    if (DBG is not None or DBG2 is not None or DBGT is not None
            or DBG3 or DBG5 is not None or DBG6 or DBG7 or DBG8):
        dbg = nc.dram_tensor("dbg", [128, 6400], F32, kind="ExternalOutput").ap()

    modsh = nc.dram_tensor("modsh", [12, 128], F32).ap()
    modag = nc.dram_tensor("modag", [96, 128], F32).ap()
    pzin = nc.dram_tensor("pzin", [128, 2 * H * DH], F32).ap()
    pzag = nc.dram_tensor("pzag", [NCORES * 128, 2 * H * DH], F32).ap()
    _kind = dict(kind="ExternalOutput") if DBG3 else {}
    dOl = nc.dram_tensor("dOl", [NCHUNK * H * CT, DH], F32, **_kind).ap()
    dQc = nc.dram_tensor("dQc", [128, NCHUNK * H * CT], F32, **_kind).ap()
    dx1 = nc.dram_tensor("dx1", [HID, LTOK], F32, **_kind).ap()

    dgg = nc.dram_tensor("dgg", [KF * 128, LTOK], F32).ap()
    d8g = nc.dram_tensor("d8g", [HID, 6144], BF16).ap()
    d8u = nc.dram_tensor("d8u", [HID, 6144], BF16).ap()
    d8d = nc.dram_tensor("d8d", [6144, HID], BF16).ap()

    ctx = ExitStack()
    ART = 52600
    ar_t = ctx.enter_context(nc.sbuf_tensor("arena", [128, ART], F32))
    A = Arena(ar_t, ART)

    xres = A.alloc(NDC * (LTOK + 3))
    ident = A.alloc(128)
    onescol = A.alloc(1)
    maskN = A.alloc(128)
    maskNT = A.alloc(128)
    maskQT = A.alloc(64)
    modfm = A.alloc(96)
    onecal = A.alloc(16)
    onecbl = A.alloc(16)
    gam = A.alloc(16)
    gbm = A.alloc(16)
    dsgb = A.alloc(16)
    dscf = A.alloc(16)
    gscf = A.alloc(48)
    uscf = A.alloc(48)
    onwrep = A.alloc(128, parts=64)
    cq_w = A.alloc(8 * 4)
    ck_w = A.alloc(16 * 4)
    cv_w = A.alloc(16 * 4)
    hzt = A.alloc(3)
    rmt = A.alloc(8)
    rmit = A.alloc(8)
    tails = A.alloc(40 * 3)
    onesr = A.alloc(128, parts=1)
    dncol = A.alloc(2, parts=8)
    epscol = A.alloc(1)
    rdhcol = A.alloc(1)

    PERSIST = A.mark()
    srow = A.alloc(2048, parts=1)
    sil_c = A.alloc(16)
    gtmp = A.alloc(32)
    nega = A.alloc(8, parts=1)
    dtbr = A.alloc(8, parts=1)
    onwrow = A.alloc(128, parts=1)

    pb = [ctx.enter_context(nc.psum_tensor(f"pb{i}", [128, 512], F32))
          for i in range(8)]

    sD = Sem(ctx, nc, "sD")
    sW = Sem(ctx, nc, "sW")
    sT = Sem(ctx, nc, "sT")
    sV = Sem(ctx, nc, "sV")
    sS = Sem(ctx, nc, "sS")
    sG = Sem(ctx, nc, "sG")
    sCC = Sem(ctx, nc, "sCC")

    blk = ctx.enter_context(nc.Block())

    def T(f): blk.tensor(f)
    def V(f): blk.vector(f)
    def S(f): blk.scalar(f)
    def G(f): blk.gpsimd(f)
    def SY(f): blk.sync(f)

    def barrier():
        def _t(e): sT.inc(e.drain())
        T(_t)
        def _v(e): sV.inc(e.drain())
        V(_v)
        def _s(e): sS.inc(e.drain())
        S(_s)
        def _g(e): sG.inc(e.drain())
        G(_g)
        def _w(e):
            for s in (sT, sV, sS, sG, sD, sW, sCC):
                s.wait(e)
        for sec in (T, V, S, G, SY):
            sec(_w)

    def taildump(phase):
        if phase is None or DBGT != phase:
            return False
        def _td(e):
            for s in (sT, sV, sS, sG, sW):
                s.wait(e)
            sD.inc(e.dma_start(out=dbg[:, 0:120], in_=tails[:, :]), 16)
            sD.inc(e.dma_start(out=dbg[:, 128:384], in_=kpe[:, 0, :]), 16)
            sD.inc(e.dma_start(out=dbg[:, 384:643], in_=uscr[:, :]), 16)
            sD.wait(e)
        SY(_td)
        return True

    def _p0_dma(e):
        for i in range(NDC):
            sD.inc(e.dma_start(out=xres[:, i * (LTOK + 3):(i + 1) * (LTOK + 3)],
                               in_=xT[i * 128:(i + 1) * 128, :]), 16)
        sD.inc(e.dma_start(out=srow[0:1, 0:2048], in_=cvec[0:1, :]), 16)
        sD.inc(e.dma_start(out=nega, in_=alog[0:1, :]), 16)
        sD.inc(e.dma_start(out=dtbr, in_=dtb[0:1, :]), 16)
        sD.inc(e.dma_start(out=hzt[:, :], in_=hz[:, :]), 16)
        sD.inc(e.dma_start(out=rmt[:, :], in_=rmask[:, :]), 16)
        sD.inc(e.dma_start(out=rmit[:, :], in_=rmaskinv[:, :]), 16)
        sD.inc(e.dma_start(out=onwrow, in_=onw[0:1, :]), 16)
        for (dst, src, nf) in ((cq_w, convq, 8), (ck_w, convk, 16), (cv_w, convv, 16)):
            d = dst.rearrange("p (f s) -> p f s", s=4)
            for f in range(nf):
                sD.inc(e.dma_start(out=d[:, f, :],
                                   in_=src[f * 128:(f + 1) * 128, :]), 16)
    SY(_p0_dma)

    def _p0_masks(g):
        g.memset(ident[:, :], 0.0)
        g.memset(maskN[:, :], 0.0)
        g.memset(maskNT[:, :], 0.0)
        g.memset(maskQT[:, :], 0.0)
        g.memset(onescol[:, :], 1.0)
        g.memset(onesr, 1.0)
        g.memset(epscol[:, :], 1e-5)
        g.memset(rdhcol[:, :], 1.0 / 128.0)
        g.drain().then_inc(sG.h, 1)
        sG.n += 1
        g.wait_ge(sG.h, sG.n)
        g.affine_select(out=ident[:, :], in_=ident[:, :], compare_op=ALU.not_equal,
                        fill=1.0, base=0, pattern=[[-1, 128]], channel_multiplier=1)
        for (jr, jp, base) in ((0, 0, -1), (0, 1, -1), (1, 0, 0), (1, 1, -1)):
            g.affine_select(out=maskN[jr * 64:jr * 64 + 64, jp * 64:jp * 64 + 64],
                            in_=maskN[jr * 64:jr * 64 + 64, jp * 64:jp * 64 + 64],
                            compare_op=ALU.is_ge, fill=NEG, base=base,
                            pattern=[[-1, 64]], channel_multiplier=1)
        for (ja, jb, base) in ((0, 0, -1), (0, 1, 0), (1, 0, -1), (1, 1, -1)):
            g.affine_select(out=maskNT[ja * 64:ja * 64 + 64, jb * 64:jb * 64 + 64],
                            in_=maskNT[ja * 64:ja * 64 + 64, jb * 64:jb * 64 + 64],
                            compare_op=ALU.is_ge, fill=NEG, base=base,
                            pattern=[[1, 64]], channel_multiplier=-1)
        for js in (0, 1):
            g.affine_select(out=maskQT[js * 64:js * 64 + 64, :],
                            in_=maskQT[js * 64:js * 64 + 64, :],
                            compare_op=ALU.is_ge, fill=NEG, base=0,
                            pattern=[[1, 64]], channel_multiplier=-1)
        g.drain().then_inc(sG.h, 1)
        sG.n += 1
    G(_p0_masks)

    def _p1_silu(e):
        sD.wait(e)
        e.activation(srow[0:1, 0:2048], srow[0:1, 0:2048], AF.Silu)
        e.activation(nega, nega, AF.Exp)
        e.mul(nega, nega, -1.0)
        sS.inc(e.drain())
    S(_p1_silu)

    def _p1_cmm(e):
        sS.wait(e)
        sG.wait(e)
        for j in range(16):
            e.matmul(pb[0][:, j:j + 1], srow[0:1, j * 128:(j + 1) * 128],
                     onescol[0:1, 0:1], start=True, stop=True)
        e.matmul(pb[0][0:8, 16:17], dtbr, onescol[0:1, 0:1], start=True, stop=True)
        e.matmul(pb[0][0:8, 17:18], nega, onescol[0:1, 0:1], start=True, stop=True)
        e.matmul(pb[1][0:64, 0:128], onesr[0:1, 0:64], onwrow, start=True, stop=True)
        sT.inc(e.drain())
    T(_p1_cmm)

    def _p1_cev(e):
        sT.wait(e)
        e.tensor_copy(sil_c[:, :], pb[0][:, 0:16])
        e.tensor_copy(dncol[0:8, :], pb[0][0:8, 16:18])
        e.tensor_copy(onwrep[:, :], pb[1][0:64, 0:128])
        sV.inc(e.drain())
    V(_p1_cev)

    NMC = 12288 // NCORES // 128
    PH = A.mark()
    wm_slab = A.alloc(2 * NMC * 128)
    mod12 = A.alloc(NMC)
    mod12t = A.alloc(128, parts=12)
    mod96 = A.alloc(128, parts=96)

    def _p1_wdma(e):
        bT = sT.n
        for dc in range(NDC):
            half = (dc % 2) * NMC * 128
            e.wait_ge(sT.h, bT + max(dc - 1, 0))
            sW.inc(e.dma_start(out=wm_slab[:, half:half + NMC * 128],
                               in_=wmodsl[dc * 128:(dc + 1) * 128, :]), 16)
    G(_p1_wdma)

    def _p1_mm2(e):
        sV.wait(e)
        for dc in range(NDC):
            half = (dc % 2) * NMC * 128
            e.wait_ge(sW.h, sW.n - 16 * (NDC - 1 - dc))
            for fc in range(NMC):
                e.matmul(pb[2][:, fc:fc + 1],
                         wm_slab[:, half + fc * 128:half + (fc + 1) * 128],
                         sil_c[:, dc:dc + 1],
                         start=(dc == 0 and fc == 0),
                         stop=(dc == NDC - 1 and fc == NMC - 1))
            sT.inc(e.drain())
    T(_p1_mm2)

    def _p1_ev2(e):
        sT.wait(e)
        sV.inc(e.tensor_copy(mod12[:, :], pb[2][:, 0:NMC]))
    V(_p1_ev2)

    def _p1_tr2(e):
        sV.wait(e)
        e.transpose(pb[3][0:12, 0:128], mod12[:, :], ident[:, :])
        sT.inc(e.drain())
    T(_p1_tr2)

    def _p1_ev3(e):
        sT.wait(e)
        sV.inc(e.tensor_copy(mod12t[:, :], pb[3][0:12, 0:128]))
    V(_p1_ev3)

    def _p1_ship(e):
        sV.wait(e)
        sD.inc(e.dma_start(out=modsh[:, :], in_=mod12t[:, :]), 16)
        sD.wait(e)
    SY(_p1_ship)

    def _p1_ag(g):
        sD.wait(g)
        sCC.inc(g.collective_compute(
            "AllGather", ALU.bypass, replica_groups=[list(range(NCORES))],
            ins=[modsh[:, :].opt()], outs=[modag[:, :].opt()]))
    G(_p1_ag)

    def _p1_back(e):
        sCC.wait(e)
        sD.inc(e.dma_start(out=mod96[:, :], in_=modag[:, :]), 16)
    SY(_p1_back)

    def _p1_tr3(e):
        sD.wait(e)
        e.transpose(pb[3][:, 0:96], mod96[:, :], ident[0:96, 0:96])
        sT.inc(e.drain())
    T(_p1_tr3)

    def _p1_ev4(e):
        sT.wait(e)
        sV.inc(e.tensor_copy(modfm[:, :], pb[3][:, 0:96]))
    V(_p1_ev4)

    def _p1_mods(e):
        sV.wait(e)
        e.tensor_scalar_add(onecal[:, :], modfm[:, 16:32], 1.0)
        e.tensor_scalar_add(onecbl[:, :], modfm[:, 64:80], 1.0)
        sV.inc(e.drain())
    V(_p1_mods)

    def _p1_cl_s(e):
        sV.wait(e)
        e.activation(gtmp[:, 0:16], modfm[:, 32:48], AF.Abs)
        e.activation(gtmp[:, 16:32], modfm[:, 80:96], AF.Abs)
        e.sign(gam[:, :], modfm[:, 32:48])
        e.sign(gbm[:, :], modfm[:, 80:96])
        sS.inc(e.drain())
    S(_p1_cl_s)

    def _p1_cl_v(e):
        sS.wait(e)
        e.tensor_scalar_max(gtmp[:, 0:16], gtmp[:, 0:16], 0.25)
        e.tensor_scalar_max(gtmp[:, 16:32], gtmp[:, 16:32], 0.25)
        e.tensor_mul(gam[:, :], gam[:, :], gtmp[:, 0:16])
        e.tensor_mul(gbm[:, :], gbm[:, :], gtmp[:, 16:32])
        sV.inc(e.drain())
    V(_p1_cl_v)

    def p1_scales(src_ap, dst, n16):
        for part in range((n16 + 15) // 16):
            w16 = min(16, n16 - part * 16)
            def _dma(e, part=part, w16=w16, src_ap=src_ap):
                sV.wait(e)
                sT.wait(e)
                sS.wait(e)
                sD.inc(e.dma_start(
                    out=srow[0:1, 0:w16 * 128],
                    in_=src_ap[0:1, part * 2048:part * 2048 + w16 * 128]), 16)
            SY(_dma)
            def _mm(e, w16=w16):
                sD.wait(e)
                for j in range(w16):
                    e.matmul(pb[4][:, j:j + 1], srow[0:1, j * 128:(j + 1) * 128],
                             onescol[0:1, 0:1], start=True, stop=True)
                sT.inc(e.drain())
            T(_mm)
            def _ev(e, part=part, w16=w16, dst=dst):
                sT.wait(e)
                sV.inc(e.tensor_copy(dst[:, part * 16:part * 16 + w16],
                                     pb[4][:, 0:w16]))
            V(_ev)
    p1_scales(gsc, gscf, 48)
    p1_scales(usc, uscf, 48)
    p1_scales(dsc, dscf, 16)

    def _p1_dsgb(e):
        sV.wait(e)
        sV.inc(e.tensor_mul(dsgb[:, :], dscf[:, :], gbm[:, :]))
    V(_p1_dsgb)

    barrier()

    A.reset(PERSIST)
    P2M = A.mark()
    xt_t = A.alloc(NDC * (MEGA + 3))
    kproj = A.alloc(2 * KF * MEGA)
    vproj = A.alloc(2 * KF * MEGA)
    qproj = A.alloc(KF * MEGA)
    uscr = A.alloc(MEGA + 3)
    gbuf = A.alloc(KF * MEGA)
    wslab = A.alloc(2 * 2048)
    bu = A.alloc(MEGA, parts=16)
    au = A.alloc(MEGA, parts=8)
    bval = A.alloc(MEGA, parts=16)
    lnb = A.alloc(MEGA, parts=16)
    gG = A.alloc(MEGA, parts=8)
    gdup = A.alloc(MEGA, parts=16)
    lnn_k = A.alloc(MEGA, parts=16)
    lnnq = A.alloc(MEGA, parts=8)
    rowexp = A.alloc(MEGA, parts=16)
    ncolexp = A.alloc(MEGA, parts=16)
    erow_t = A.alloc(MEGA, parts=16)
    qrow_t = A.alloc(MEGA, parts=8)
    exprow = A.alloc(MEGA, parts=16)
    experow = A.alloc(MEGA, parts=16)
    expcol = A.alloc(MEGA, parts=16)
    expqrow = A.alloc(MEGA, parts=8)
    zrow = A.alloc(64, parts=8)
    lnrow = A.alloc(2 * (MEGA + 3), parts=1)
    stg = A.alloc(512, parts=1)
    r2a = A.alloc(1024, parts=2)
    r2b = A.alloc(1024, parts=2)
    r2q = A.alloc(512, parts=2)
    colT = A.alloc(32)
    qcols = A.alloc(8, parts=64)
    gcols = A.alloc(8)
    dtmp = A.alloc(128)
    dtmp2 = A.alloc(128)
    dqtmp = A.alloc(64)
    nA_ = A.alloc(128)
    nAT_ = A.alloc(128)
    nB_ = A.alloc(128)
    nBT_ = A.alloc(128)
    rhs = A.alloc(256)
    ke = A.alloc(128)
    aqk = A.alloc(64)
    qtsb = A.alloc(128, parts=64)
    qeff = A.alloc(128, parts=64)
    qefft = A.alloc(64)
    ptile = A.alloc(128)
    pttile = A.alloc(128)
    sloc = A.alloc(H * DH)
    ppre = A.alloc(H * DH)
    A.alloc(128)
    oltmp = A.sub(128, 64, 0)
    qct = A.alloc(64)
    xsq = A.alloc(MEGA + 3)
    kctile = A.alloc(128)
    vctile = A.alloc(128)
    assert A.off <= ART, A.off

    def _z(g):
        g.memset(zrow, 0.0)
        g.memset(r2a[0:2, :], 1.0)
        g.memset(r2b[0:2, :], 1.0)
        g.memset(r2q[0:2, :], 1.0)
        g.memset(sloc[:, :], 0.0)
        g.memset(ppre[:, :], 0.0)
        g.drain().then_inc(sG.h, 1)
        sG.n += 1
    G(_z)
    barrier()

    xre = xres.rearrange("p (d t) -> p d t", d=NDC)
    xte = xt_t.rearrange("p (d t) -> p d t", d=NDC)
    kpe = kproj.rearrange("p (f t) -> p f t", f=2 * KF)
    vpe = vproj.rearrange("p (f t) -> p f t", f=2 * KF)
    qpe = qproj.rearrange("p (f t) -> p f t", f=KF)

    for mg in range(NMEGA):
        t0 = mg * MEGA
        W = MEGA + 3

        for dc in range(NDC):
            def _sq(e, dc=dc, t0=t0, W=W):
                if dc == 0:
                    sV.wait(e)
                else:
                    sT.wait(e)
                e.activation(xsq[:, :], xre[:, dc, t0:t0 + W], AF.Square)
                sS.inc(e.drain())
            S(_sq)
            def _mm(e, dc=dc, t0=t0, W=W):
                sS.wait(e)
                e.matmul(pb[0][0:1, 0:W], onescol[:, 0:1], xre[:, dc, t0:t0 + W],
                         start=(dc == 0), stop=(dc == NDC - 1))
                e.matmul(pb[1][0:1, 0:W], onescol[:, 0:1], xsq[:, :],
                         start=(dc == 0), stop=(dc == NDC - 1))
                sT.inc(e.drain())
            T(_mm)

        def _ln_v(e, W=W):
            sT.wait(e)
            e.tensor_scalar_mul(lnrow[0:1, 0:W], pb[0][0:1, 0:W], 1.0 / HID)
            e.tensor_scalar_mul(lnrow[0:1, W:2 * W], pb[1][0:1, 0:W], 1.0 / HID)
            e.tensor_mul(xsq[0:1, 0:W], lnrow[0:1, 0:W], lnrow[0:1, 0:W])
            e.tensor_sub(lnrow[0:1, W:2 * W], lnrow[0:1, W:2 * W], xsq[0:1, 0:W])
            sV.inc(e.drain())
        V(_ln_v)
        def _ln_s(e, W=W):
            sV.wait(e)
            e.activation(lnrow[0:1, W:2 * W], lnrow[0:1, W:2 * W], AF.Sqrt,
                         bias=epscol[0:1, 0:1])
            sS.inc(e.drain())
        S(_ln_s)
        def _ln_v2(e, W=W):
            sS.wait(e)
            sV.inc(e.reciprocal(lnrow[0:1, W:2 * W], lnrow[0:1, W:2 * W]))
        V(_ln_v2)
        def _ln_rep(e, W=W):
            sV.wait(e)
            e.matmul(pb[2][:, 0:W], onesr[0:1, 0:128], lnrow[0:1, 0:W],
                     start=True, stop=True)
            e.matmul(pb[3][:, 0:W], onesr[0:1, 0:128], lnrow[0:1, W:2 * W],
                     start=True, stop=True)
            sT.inc(e.drain())
        T(_ln_rep)
        for dc in range(NDC):
            def _m(e, dc=dc, mg=mg, t0=t0, W=W):
                if dc == 0:
                    sT.wait(e)
                e.tensor_sub(xte[:, dc, :], xre[:, dc, t0:t0 + W], pb[2][:, 0:W])
                e.tensor_mul(xte[:, dc, :], xte[:, dc, :], pb[3][:, 0:W])
                e.tensor_scalar(xte[:, dc, :], xte[:, dc, :],
                                onecal[:, dc:dc + 1], modfm[:, dc:dc + 1],
                                ALU.mult, ALU.add)
                if mg == 0:
                    e.tensor_mul(xte[:, dc, 0:3], xte[:, dc, 0:3], hzt[:, :])
                if dc == NDC - 1:
                    sV.inc(e.drain())
            V(_m)

        def wstream(wt, width):
            def _wd(e, wt=wt, width=width):
                bT = sT.n
                for dc in range(NDC):
                    half = (dc % 2) * 2048
                    e.wait_ge(sT.h, bT + max(dc - 1, 0))
                    sW.inc(e.dma_start(out=wslab[:, half:half + width],
                                       in_=wt[dc * 128:(dc + 1) * 128, :]), 16)
            G(_wd)

        def proj_mm(nfc):
            def _mm(e, nfc=nfc):
                sV.wait(e)
                for dc in range(NDC):
                    half = (dc % 2) * 2048
                    e.wait_ge(sW.h, sW.n - 16 * (NDC - 1 - dc))
                    for fc in range(nfc):
                        e.matmul(pb[fc // 2][:, (fc % 2) * 256:(fc % 2) * 256 + MEGA],
                                 wslab[:, half + fc * 128:half + (fc + 1) * 128],
                                 xte[:, dc, 3:3 + MEGA],
                                 start=(dc == 0 and fc % 2 == 0),
                                 stop=(dc == NDC - 1 and fc % 2 == 1))
                    sT.inc(e.drain())
            T(_mm)

        def conv_silu(nfc, dst, conv_w, tail_off):
            for fc in range(nfc):
                def _v(e, fc=fc, dst=dst, conv_w=conv_w, tail_off=tail_off):
                    if fc == 0:
                        sT.wait(e)
                    tl = tails[:, (tail_off + fc) * 3:(tail_off + fc) * 3 + 3]
                    e.tensor_copy(uscr[:, 0:3], tl)
                    e.tensor_copy(uscr[:, 3:3 + MEGA],
                                  pb[fc // 2][:, (fc % 2) * 256:(fc % 2) * 256 + MEGA])
                    e.tensor_copy(tl, pb[fc // 2][:, (fc % 2) * 256 + MEGA - 3:
                                                  (fc % 2) * 256 + MEGA])
                    wc = conv_w.rearrange("p (f s) -> p f s", s=4)
                    e.tensor_scalar_mul(dst[:, fc, :], uscr[:, 3:3 + MEGA],
                                        wc[:, fc, 3:4])
                    for s in (2, 1, 0):
                        e.scalar_tensor_tensor(dst[:, fc, :], uscr[:, s:s + MEGA],
                                               wc[:, fc, s:s + 1], dst[:, fc, :],
                                               ALU.mult, ALU.add)
                    sV.inc(e.drain())
                V(_v)
                def _s(e, fc=fc, dst=dst):
                    sV.wait(e)
                    e.activation(dst[:, fc, :], dst[:, fc, :], AF.Silu)
                    sS.inc(e.drain())
                S(_s)

        def halo_proj(wt, nfc, tail_off):
            if mg != 0:
                return
            wstream(wt, nfc * 128)
            def _mm(e, nfc=nfc):
                sV.wait(e)
                for dc in range(NDC):
                    half = (dc % 2) * 2048
                    e.wait_ge(sW.h, sW.n - 16 * (NDC - 1 - dc))
                    for fc in range(nfc):
                        e.matmul(pb[6][:, fc * 3:fc * 3 + 3],
                                 wslab[:, half + fc * 128:half + (fc + 1) * 128],
                                 xte[:, dc, 0:3],
                                 start=(dc == 0 and fc == 0),
                                 stop=(dc == NDC - 1 and fc == nfc - 1))
                    sT.inc(e.drain())
            T(_mm)
            def _ev(e, nfc=nfc, tail_off=tail_off):
                sT.wait(e)
                e.tensor_copy(tails[:, tail_off * 3:(tail_off + nfc) * 3],
                              pb[6][:, 0:nfc * 3])
                sV.inc(e.drain())
            V(_ev)

        halo_proj(wkT, 16, 0)
        wstream(wkT, 2048)
        proj_mm(16)
        conv_silu(16, kpe, ck_w, 0)
        if taildump(1 if mg == 0 else None):
            ctx.close()
            return nc
        halo_proj(wvT, 16, 16)
        wstream(wvT, 2048)
        proj_mm(16)
        conv_silu(16, vpe, cv_w, 16)
        if taildump(2 if mg == 0 else None):
            ctx.close()
            return nc
        halo_proj(wqT, 8, 32)
        wstream(wqT, 1024)
        proj_mm(8)
        conv_silu(8, qpe, cq_w, 32)

        if taildump(3 if mg == 0 else None):
            ctx.close()
            return nc
        def _ba_wd(e):
            sT.wait(e)
            sW.inc(e.dma_start(
                out=wslab[:, 0:384].rearrange("p (d f) -> p d f", d=NDC),
                in_=wbaT[:, :].rearrange("(d p) f -> p d f", p=128)), 16)
        G(_ba_wd)
        def _ba_mm(e):
            sW.wait(e)
            sV.wait(e)
            for dc in range(NDC):
                e.matmul(pb[6][0:16, 0:MEGA], wslab[:, dc * 24:dc * 24 + 16],
                         xte[:, dc, 3:3 + MEGA],
                         start=(dc == 0), stop=(dc == NDC - 1))
                e.matmul(pb[7][0:8, 0:MEGA], wslab[:, dc * 24 + 16:dc * 24 + 24],
                         xte[:, dc, 3:3 + MEGA],
                         start=(dc == 0), stop=(dc == NDC - 1))
            sT.inc(e.drain())
        T(_ba_mm)
        def _ba_ev(e):
            sT.wait(e)
            e.tensor_copy(bu[0:16, :], pb[6][0:16, 0:MEGA])
            e.tensor_copy(au[0:8, :], pb[7][0:8, 0:MEGA])
            sV.inc(e.drain())
        V(_ba_ev)

        wstream(wgT, 1024)
        proj_mm(8)
        def _g_ev(e, t0=t0):
            sT.wait(e)
            sD.wait(e)
            gb = gbuf.rearrange("p (f t) -> p f t", f=KF)
            for fc in range(KF):
                e.tensor_copy(gb[:, fc, :],
                              pb[fc // 2][:, (fc % 2) * 256:(fc % 2) * 256 + MEGA])
            sV.inc(e.drain())
        V(_g_ev)
        def _g_st(e, t0=t0):
            sV.wait(e)
            for fc in range(KF):
                sD.inc(e.dma_start(
                    out=dgg[fc * 128:(fc + 1) * 128, t0:t0 + MEGA],
                    in_=gbuf.rearrange("p (f t) -> p f t", f=KF)[:, fc, :]), 16)
        SY(_g_st)

        def _b_s1(e):
            sV.wait(e)
            e.activation(bval[0:16, :], bu[0:16, :], AF.Sigmoid)
            e.activation(lnb[0:16, :], bu[0:16, :], AF.Abs)
            sS.inc(e.drain())
        S(_b_s1)
        def _b_s2(e):
            sS.wait(e)
            e.activation(lnb[0:16, :], lnb[0:16, :], AF.Exp, scale=-1.0)
            sS.inc(e.drain())
        S(_b_s2)
        def _b_s3(e):
            sS.wait(e)
            e.activation(lnb[0:16, :], lnb[0:16, :], AF.Ln, bias=1.0)
            sS.inc(e.drain())
        S(_b_s3)
        def _b_v(e):
            sS.wait(e)
            e.tensor_scalar_mul(bval[0:16, :], bval[0:16, :], 2.0)
            e.tensor_scalar(xsq[0:16, 0:MEGA], bu[0:16, :],
                            -1.0, 0.0, ALU.mult, ALU.max)
            e.tensor_add(lnb[0:16, :], lnb[0:16, :], xsq[0:16, 0:MEGA])
            e.tensor_scalar(lnb[0:16, :], lnb[0:16, :],
                            -1.0, math.log(2.0), ALU.mult, ALU.add)
            sV.inc(e.drain())
        V(_b_v)

        def _g_v1(e):
            sV.wait(e)
            e.tensor_scalar_add(au[0:8, :], au[0:8, :], dncol[0:8, 0:1])
            sV.inc(e.drain())
        V(_g_v1)
        def _g_s1(e):
            sV.wait(e)
            e.activation(xsq[0:8, 0:MEGA], au[0:8, :], AF.Abs)
            sS.inc(e.drain())
        S(_g_s1)
        def _g_s2(e):
            sS.wait(e)
            e.activation(xsq[0:8, 0:MEGA], xsq[0:8, 0:MEGA], AF.Exp, scale=-1.0)
            sS.inc(e.drain())
        S(_g_s2)
        def _g_s3(e):
            sS.wait(e)
            e.activation(xsq[0:8, 0:MEGA], xsq[0:8, 0:MEGA], AF.Ln, bias=1.0)
            sS.inc(e.drain())
        S(_g_s3)
        def _g_v2(e):
            sS.wait(e)
            e.tensor_scalar_max(au[0:8, :], au[0:8, :], 0.0)
            e.tensor_add(au[0:8, :], au[0:8, :], xsq[0:8, 0:MEGA])
            e.tensor_scalar_mul(au[0:8, :], au[0:8, :], dncol[0:8, 1:2])
            for cc in range(CPM):
                e.tensor_tensor_scan(gG[0:8, cc * CT:(cc + 1) * CT],
                                     au[0:8, cc * CT:(cc + 1) * CT],
                                     zrow[0:8, 0:CT], 0.0, ALU.add, ALU.add)
            sV.inc(e.drain())
        V(_g_v2)
        def _g_d(e):
            sV.wait(e)
            sD.inc(e.dma_start(out=gdup[0:8, :], in_=gG[0:8, :]), 16)
            sD.inc(e.dma_start(out=gdup[8:16, :], in_=gG[0:8, :]), 16)
        SY(_g_d)

        if taildump(4 if mg == 0 else None):
            ctx.close()
            return nc
        srcs = [(kpe, fc) for fc in range(16)] + [(qpe, fc) for fc in range(8)]
        for pr in range(12):
            for k2 in range(2):
                gfc = pr * 2 + k2
                src, fc = srcs[gfc]
                def _sq(e, src=src, fc=fc, gfc=gfc):
                    sD.wait(e)
                    if gfc == 0:
                        sV.wait(e)
                    else:
                        sT.wait(e)
                    e.activation(xsq[:, 0:MEGA], src[:, fc, :], AF.Square)
                    sS.inc(e.drain())
                S(_sq)
                def _mm(e, k2=k2):
                    sS.wait(e)
                    e.matmul(pb[6][0:1, k2 * 256:k2 * 256 + MEGA],
                             onescol[:, 0:1], xsq[:, 0:MEGA],
                             start=True, stop=True)
                    sT.inc(e.drain())
                T(_mm)
            def _ev(e):
                sT.wait(e)
                sV.inc(e.tensor_copy(stg[0:1, 0:512], pb[6][0:1, 0:512]))
            V(_ev)
            def _sc(e, pr=pr):
                sV.wait(e)
                if pr < 8:
                    d = lnn_k[2 * pr:2 * pr + 2, :]
                else:
                    d = lnnq[2 * (pr - 8):2 * (pr - 8) + 2, :]
                sD.inc(e.dma_start(
                    out=d, in_=stg[0:1, 0:512].rearrange("p (a t) -> p a t", a=2)), 16)
            SY(_sc)
        def _n_v(e):
            sD.wait(e)
            e.tensor_scalar_max(lnn_k[0:16, :], lnn_k[0:16, :], 1e-24)
            e.tensor_scalar_max(lnnq[0:8, :], lnnq[0:8, :], 1e-24)
            sV.inc(e.drain())
        V(_n_v)
        def _n_s(e):
            sV.wait(e)
            e.activation(lnn_k[0:16, :], lnn_k[0:16, :], AF.Ln)
            e.activation(lnnq[0:8, :], lnnq[0:8, :], AF.Ln)
            sS.inc(e.drain())
        S(_n_s)
        def _n_v2(e):
            sS.wait(e)
            e.tensor_scalar_mul(lnn_k[0:16, :], lnn_k[0:16, :], 0.5)
            e.tensor_scalar_mul(lnnq[0:8, :], lnnq[0:8, :], 0.5)
            sV.inc(e.drain())
        V(_n_v2)

        def _rows2(e):
            sV.wait(e)
            gsrc = gdup[0:16, :].rearrange("p (c l) -> p c l", c=CPM)
            for cc in range(CPM):
                e.tensor_copy(xsq[0:16, cc * CT:(cc + 1) * CT],
                              gsrc[:, cc, CT - 1:CT].broadcast_to([16, CT]))
            e.tensor_sub(erow_t[0:16, :], xsq[0:16, 0:MEGA], gdup[0:16, :])
            e.tensor_sub(erow_t[0:16, :], erow_t[0:16, :], lnn_k[0:16, :])
            e.scalar_tensor_tensor(qrow_t[0:8, :], gdup[0:8, :], 1.0,
                                   lnnq[0:8, :], ALU.mult, ALU.subtract)
            e.tensor_scalar_add(qrow_t[0:8, :], qrow_t[0:8, :], -LNDH_HALF)
            e.tensor_sub(rowexp[0:16, :], lnb[0:16, :], lnn_k[0:16, :])
            e.tensor_add(rowexp[0:16, :], rowexp[0:16, :], gdup[0:16, :])
            e.tensor_add(ncolexp[0:16, :], gdup[0:16, :], lnn_k[0:16, :])
            e.tensor_scalar_mul(ncolexp[0:16, :], ncolexp[0:16, :], -1.0)
            sV.inc(e.drain())
        V(_rows2)
        def _v1s(e):
            sV.wait(e)
            e.activation(exprow[0:16, :], rowexp[0:16, :], AF.Exp)
            e.activation(experow[0:16, :], erow_t[0:16, :], AF.Exp)
            e.activation(expcol[0:16, :], ncolexp[0:16, :], AF.Exp, scale=-1.0)
            e.activation(expqrow[0:8, :], qrow_t[0:8, :], AF.Exp)
            sS.inc(e.drain())
        S(_v1s)

        if taildump(5 if mg == 0 else None):
            ctx.close()
            return nc
        for cc in range(CPM):
            cg = mg * CPM + cc
            ts = t0 + cc * CT

            tl0 = cc * CT
            def _r2d(e, tl0=tl0):
                sV.wait(e)
                sS.wait(e)
                da = r2a[0:1, :].rearrange("p (h j l) -> p h j l", h=8, j=2)
                sD.inc(e.dma_start(out=da[:, :, 0, :], in_=rowexp[0:8, tl0:tl0 + CT]), 16)
                sD.inc(e.dma_start(out=da[:, :, 1, :], in_=rowexp[8:16, tl0:tl0 + CT]), 16)
                db = r2b[1:2, :].rearrange("p (h j l) -> p h j l", h=8, j=2)
                sD.inc(e.dma_start(out=db[:, :, 0, :], in_=ncolexp[0:8, tl0:tl0 + CT]), 16)
                sD.inc(e.dma_start(out=db[:, :, 1, :], in_=ncolexp[8:16, tl0:tl0 + CT]), 16)
                dq = r2q[0:1, :].rearrange("p (h l) -> p h l", h=8)
                sD.inc(e.dma_start(out=dq[:, :, :], in_=qrow_t[0:8, tl0:tl0 + CT]), 16)
            SY(_r2d)
            def _ctr(e, tl0=tl0):
                sV.wait(e)
                sS.wait(e)
                e.transpose(pb[7][0:64, 0:16], exprow[0:16, tl0:tl0 + CT],
                            ident[0:16, 0:16])
                e.transpose(pb[7][0:64, 16:32], bval[0:16, tl0:tl0 + CT],
                            ident[0:16, 0:16])
                e.transpose(pb[7][0:64, 32:48], experow[0:16, tl0:tl0 + CT],
                            ident[0:16, 0:16])
                e.transpose(pb[7][0:64, 48:64], expcol[0:16, tl0:tl0 + CT],
                            ident[0:16, 0:16])
                e.transpose(pb[7][0:64, 64:72], expqrow[0:8, tl0:tl0 + CT],
                            ident[0:8, 0:8])
                sT.inc(e.drain())
            T(_ctr)
            def _ctv(e):
                sT.wait(e)
                src = pb[7][0:64, 0:64].rearrange("p (q j h) -> p q j h", q=4, j=2)
                cl = colT.rearrange("p (q h) -> p q h", q=4)
                e.tensor_copy(cl[0:64, :, :], src[:, :, 0, :])
                e.tensor_copy(colT[64:128, :].rearrange("p (q h) -> p q h", q=4),
                              src[:, :, 1, :])
                e.tensor_copy(qcols[0:64, :], pb[7][0:64, 64:72])
                e.tensor_mul(gcols[:, :], colT[:, 16:24], colT[:, 24:32])
                sV.inc(e.drain())
            V(_ctv)

            for h in range(H):
                kap = kpe[:, h:h + 9:8, cc * CT:(cc + 1) * CT]
                vap = vpe[:, h:h + 9:8, cc * CT:(cc + 1) * CT]
                qap = qpe[:, h, cc * CT:(cc + 1) * CT]
                bk = pb[4 + (h % 2)]
                bt = pb[6]

                def _kvc(e, kap=kap, vap=vap):
                    sV.wait(e)
                    kcv = kctile.rearrange("p (j l) -> p j l", j=2)
                    vcv = vctile.rearrange("p (j l) -> p j l", j=2)
                    e.tensor_copy(kcv, kap)
                    e.tensor_copy(vcv, vap)
                    sV.inc(e.drain())
                V(_kvc)

                def _mm1(e, h=h, qap=qap, bk=bk):
                    sV.wait(e)
                    if h == 0:
                        sD.wait(e)
                    e.matmul(bk[:, 0:128], kctile[:, :], kctile[:, :],
                             start=True, stop=True)
                    e.matmul(bk[:, 128:192], kctile[:, :], qap,
                             start=True, stop=True)
                    e.matmul(bk[:, 192:320], r2a[0:2, h * 128:(h + 1) * 128],
                             r2b[0:2, h * 128:(h + 1) * 128], start=True, stop=True)
                    e.matmul(bk[:, 320:448], r2b[0:2, h * 128:(h + 1) * 128],
                             r2a[0:2, h * 128:(h + 1) * 128], start=True, stop=True)
                    e.matmul(bk[:, 448:512], r2b[0:2, h * 128:(h + 1) * 128],
                             r2q[0:2, h * 64:(h + 1) * 64], start=True, stop=True)
                    sT.inc(e.drain())
                T(_mm1)
                def _dv1(e, bk=bk):
                    sT.wait(e)
                    e.tensor_add(dtmp[:, :], bk[:, 192:320], maskN[:, :])
                    e.tensor_add(dtmp2[:, :], bk[:, 320:448], maskNT[:, :])
                    e.tensor_add(dqtmp[:, :], bk[:, 448:512], maskQT[:, :])
                    sV.inc(e.drain())
                V(_dv1)
                def _ds1(e):
                    sV.wait(e)
                    e.activation(dtmp[:, :], dtmp[:, :], AF.Exp)
                    e.activation(dtmp2[:, :], dtmp2[:, :], AF.Exp)
                    e.activation(dqtmp[:, :], dqtmp[:, :], AF.Exp)
                    sS.inc(e.drain())
                S(_ds1)
                def _dv2(e, bk=bk):
                    sS.wait(e)
                    e.tensor_mul(nA_[:, :], bk[:, 0:128], dtmp[:, :])
                    e.tensor_mul(nAT_[:, :], bk[:, 0:128], dtmp2[:, :])
                    e.tensor_mul(aqk[:, :], bk[:, 128:192], dqtmp[:, :])
                    sV.inc(e.drain())
                V(_dv2)
                if DBG == (mg, cc, h):
                    def _dbgv(e, bk=bk):
                        sV.wait(e)
                        e.tensor_copy(xsq[:, 0:128], bk[:, 0:128])
                        sV.inc(e.drain())
                    V(_dbgv)
                    def _dbgd(e):
                        sV.wait(e)
                        for i, (t, w) in enumerate(((nA_, 128), (nAT_, 128),
                                                    (dtmp, 128), (dtmp2, 128),
                                                    (kctile, 128), (colT, 32),
                                                    (ident, 128))):
                            sD.inc(e.dma_start(out=dbg[:, i * 128:i * 128 + w],
                                               in_=t[:, 0:w]), 16)
                        sD.inc(e.dma_start(out=dbg[:, 896:1024], in_=xsq[:, 0:128]), 16)
                        sD.inc(e.dma_start(out=dbg[0:2, 1024:2048], in_=r2a[0:2, :]), 16)
                        sD.inc(e.dma_start(out=dbg[0:2, 2048:3072], in_=r2b[0:2, :]), 16)
                        sD.inc(e.dma_start(out=dbg[0:16, 3072:3072 + 256],
                                           in_=rowexp[0:16, :]), 16)
                        sD.inc(e.dma_start(out=dbg[0:16, 3328:3328 + 256],
                                           in_=ncolexp[0:16, :]), 16)
                        sD.inc(e.dma_start(out=dbg[0:16, 3584:3584 + 256],
                                           in_=lnn_k[0:16, :]), 16)
                        sD.inc(e.dma_start(out=dbg[0:16, 3840:3840 + 256],
                                           in_=gdup[0:16, :]), 16)
                        sD.inc(e.dma_start(out=dbg[:, 4096:4192], in_=modfm[:, :]), 16)
                        sD.inc(e.dma_start(out=dbg[:, 4192:4451],
                                           in_=xte[:, 0, :]), 16)
                        sD.inc(e.dma_start(out=dbg[:, 4451:4710],
                                           in_=xte[:, 15, :]), 16)
                        sD.inc(e.dma_start(out=dbg[0:8, 4710:4966], in_=au[0:8, :]), 16)
                        sD.inc(e.dma_start(out=dbg[0:16, 4966:5222], in_=bu[0:16, :]), 16)
                        sD.inc(e.dma_start(out=dbg[0:8, 5222:5478], in_=gG[0:8, :]), 16)
                        sD.inc(e.dma_start(out=dbg[0:8, 5478:5480], in_=dncol[0:8, :]), 16)
                        sD.inc(e.dma_start(out=dbg[0:16, 5480:5736], in_=lnb[0:16, :]), 16)
                        sD.inc(e.dma_start(out=dbg[0:16, 5736:5992], in_=bval[0:16, :]), 16)
                        sD.inc(e.dma_start(out=dbg[:, 5992:6056],
                                           in_=kpe[:, 0, 0:64]), 16)
                        sD.inc(e.dma_start(out=dbg[:, 6056:6120],
                                           in_=kpe[:, 8, 0:64]), 16)
                        sD.inc(e.dma_start(out=dbg[:, 6120:6240],
                                           in_=tails[:, :]), 16)
                        sD.wait(e)
                    SY(_dbgd)
                    ctx.close()
                    return nc
                def _tr(e, qap=qap, bt=bt):
                    sV.wait(e)
                    e.transpose(bt[:, 0:128], kctile[:, :], ident[:, :])
                    e.transpose(bt[:, 128:256], vctile[:, :], ident[:, :])
                    e.transpose(bt[0:64, 256:384], qap, ident[:, :])
                    sT.inc(e.drain())
                T(_tr)
                def _rhsv(e, h=h, bt=bt):
                    sT.wait(e)
                    e.tensor_scalar_mul(rhs[:, 0:128], bt[:, 128:256],
                                        colT[:, 8 + h:9 + h])
                    e.tensor_scalar_mul(rhs[:, 128:256], bt[:, 0:128],
                                        colT[:, 0 + h:1 + h])
                    e.tensor_scalar_mul(ke[:, :], bt[:, 0:128],
                                        colT[:, 16 + h:17 + h])
                    e.tensor_copy(qtsb[0:64, :], bt[0:64, 256:384])
                    sV.inc(e.drain())
                V(_rhsv)
                cur = [nA_, nAT_]
                nxt = [nB_, nBT_]
                for j in range(7):
                    def _ap(e, cur=tuple(cur), j=j):
                        sV.wait(e)
                        e.matmul(pb[j % 2][:, 0:256], cur[1][:, :], rhs[:, :],
                                 start=True, stop=True)
                        sT.inc(e.drain())
                    T(_ap)
                    if j < 6:
                        def _sqm(e, cur=tuple(cur)):
                            e.matmul(pb[2][:, 0:128], cur[1][:, :], cur[0][:, :],
                                     start=True, stop=True)
                            e.matmul(pb[2][:, 128:256], cur[0][:, :], cur[1][:, :],
                                     start=True, stop=True)
                            sT.inc(e.drain())
                        T(_sqm)
                    def _sub(e, j=j, nxt=tuple(nxt)):
                        sT.wait(e)
                        e.tensor_sub(rhs[:, :], rhs[:, :], pb[j % 2][:, 0:256])
                        if j < 6:
                            e.tensor_copy(nxt[0][:, :], pb[2][:, 0:128])
                            e.tensor_copy(nxt[1][:, :], pb[2][:, 128:256])
                        sV.inc(e.drain())
                    V(_sub)
                    cur, nxt = nxt, cur
                def _pm(e, bt=bt):
                    sV.wait(e)
                    e.matmul(bt[:, 384:512], ke[:, :], rhs[:, 128:256],
                             start=True, stop=True)
                    e.matmul(pb[3][:, 0:128], rhs[:, 128:256], ke[:, :],
                             start=True, stop=True)
                    e.matmul(pb[3][0:64, 128:256], aqk[:, :], rhs[:, 128:256],
                             start=True, stop=True)
                    sT.inc(e.drain())
                T(_pm)
                def _pv(e, h=h, bt=bt):
                    sT.wait(e)
                    e.scalar_tensor_tensor(ptile[:, :], ident[:, :],
                                           gcols[:, h:h + 1], bt[:, 384:512],
                                           ALU.mult, ALU.subtract)
                    e.scalar_tensor_tensor(pttile[:, :], ident[:, :],
                                           gcols[:, h:h + 1], pb[3][:, 0:128],
                                           ALU.mult, ALU.subtract)
                    e.scalar_tensor_tensor(qeff[0:64, :], qtsb[0:64, :],
                                           qcols[0:64, h:h + 1],
                                           pb[3][0:64, 128:256],
                                           ALU.mult, ALU.subtract)
                    sV.inc(e.drain())
                V(_pv)
                def _tm2(e, bt=bt):
                    sV.wait(e)
                    e.transpose(bt[:, 256:320], qeff[0:64, :], ident[0:64, 0:64])
                    sT.inc(e.drain())
                T(_tm2)
                def _tv2(e, bt=bt):
                    sT.wait(e)
                    sV.inc(e.tensor_copy(qefft[:, :], bt[:, 256:320]))
                V(_tv2)
                def _tm3(e, h=h, cg=cg):
                    sV.wait(e)
                    e.matmul(pb[3][0:64, 192:320], aqk[:, :], rhs[:, 0:128],
                             start=True, stop=False)
                    e.matmul(pb[3][0:64, 192:320], qefft[:, :],
                             sloc[:, h * DH:(h + 1) * DH], start=False, stop=True)
                    if cg > 0:
                        e.matmul(pb[3][:, 320:384], ppre[:, h * DH:(h + 1) * DH],
                                 qefft[:, :], start=True, stop=True)
                    sT.inc(e.drain())
                T(_tm3)
                def _tv3(e, cg=cg):
                    sT.wait(e)
                    e.tensor_copy(oltmp[0:64, :], pb[3][0:64, 192:320])
                    if cg > 0:
                        e.tensor_copy(qct[:, :], pb[3][:, 320:384])
                    else:
                        e.tensor_copy(qct[:, :], qefft[:, :])
                    sV.inc(e.drain())
                V(_tv3)
                def _dig(e, h=h, cg=cg):
                    sV.wait(e)
                    bi = cg * H + h
                    sD.inc(e.dma_start(out=dOl[bi * CT:(bi + 1) * CT, :],
                                       in_=oltmp[0:64, :]), 16)
                    sD.inc(e.dma_start(out=dQc[:, bi * CT:(bi + 1) * CT],
                                       in_=qct[:, :]), 16)
                SY(_dig)
                def _tm4(e, h=h, cg=cg):
                    sV.wait(e)
                    e.matmul(pb[3][:, 384:512], pttile[:, :],
                             sloc[:, h * DH:(h + 1) * DH], start=True, stop=False)
                    e.matmul(pb[3][:, 384:512], ke[:, :], rhs[:, 0:128],
                             start=False, stop=True)
                    if cg > 0:
                        e.matmul(pb[2][:, 256:384], pttile[:, :],
                                 ppre[:, h * DH:(h + 1) * DH], start=True, stop=True)
                    sT.inc(e.drain())
                T(_tm4)
                def _tv4(e, h=h, cg=cg):
                    sT.wait(e)
                    sD.wait(e)
                    e.tensor_copy(sloc[:, h * DH:(h + 1) * DH], pb[3][:, 384:512])
                    if cg > 0:
                        e.tensor_copy(ppre[:, h * DH:(h + 1) * DH],
                                      pb[2][:, 256:384])
                    else:
                        e.tensor_copy(ppre[:, h * DH:(h + 1) * DH], ptile[:, :])
                    sV.inc(e.drain())
                V(_tv4)
                if DBG2 == (mg, cc, h):
                    def _dbg2(e):
                        sV.wait(e)
                        for i, (t, w) in enumerate(((rhs, 256), (ke, 128),
                                                   (aqk, 64), (qefft, 64),
                                                   (ptile, 128), (pttile, 128))):
                            sD.inc(e.dma_start(out=dbg[:, i * 256:i * 256 + w],
                                               in_=t[:, 0:w]), 16)
                        sD.inc(e.dma_start(out=dbg[:, 1536:2560], in_=sloc[:, :]), 16)
                        sD.inc(e.dma_start(out=dbg[:, 2560:3584], in_=ppre[:, :]), 16)
                        sD.inc(e.dma_start(out=dbg[0:64, 3584:3712],
                                           in_=oltmp[0:64, :]), 16)
                        sD.inc(e.dma_start(out=dbg[:, 3712:3776], in_=qct[:, :]), 16)
                        sD.inc(e.dma_start(out=dbg[0:64, 3776:3904],
                                           in_=qeff[0:64, :]), 16)
                        sD.inc(e.dma_start(out=dbg[0:64, 3904:4032],
                                           in_=qtsb[0:64, :]), 16)
                        sD.inc(e.dma_start(out=dbg[:, 4032:4040],
                                           in_=gcols[:, :]), 16)
                        sD.inc(e.dma_start(out=dbg[0:64, 4040:4048],
                                           in_=qcols[0:64, :]), 16)
                        sD.wait(e)
                    SY(_dbg2)
                    ctx.close()
                    return nc

    barrier()

    def _p3_out(e):
        sD.inc(e.dma_start(out=pzin[:, 0:H * DH], in_=ppre[:, :]), 16)
        sD.inc(e.dma_start(out=pzin[:, H * DH:2 * H * DH], in_=sloc[:, :]), 16)
        sD.wait(e)
    SY(_p3_out)
    def _p3_ag(g):
        sD.wait(g)
        sCC.inc(g.collective_compute(
            "AllGather", ALU.bypass, replica_groups=[list(range(NCORES))],
            ins=[pzin[:, :].opt()], outs=[pzag[:, :].opt()]))
    G(_p3_ag)

    A.reset(P2M)
    pzbuf = A.alloc(2 * H * DH)
    putile = A.alloc(128)
    putt = A.alloc(128)
    zutile = A.alloc(128)
    s_in = A.alloc(H * DH)
    def _p3_zero(g):
        sCC.wait(g)
        g.memset(s_in[:, :], 0.0)
        g.drain().then_inc(sG.h, 1)
        sG.n += 1
    G(_p3_zero)
    for j in range(NCORES - 1):
        def _ld(e, j=j):
            sCC.wait(e)
            sV.wait(e)
            sD.inc(e.dma_start(out=pzbuf[:, :],
                               in_=pzag[j * 128:(j + 1) * 128, :]), 16)
        SY(_ld)
        def _iminv(e, j=j):
            sG.wait(e)
            sV.inc(e.tensor_scalar_mul(zutile[:, :], ident[:, :], rmit[:, j:j + 1]))
        V(_iminv)
        iminv_done = sV.n
        for h in range(H):
            def _mask(e, j=j, h=h):
                sD.wait(e)
                sT.wait(e)
                e.scalar_tensor_tensor(putile[:, :], pzbuf[:, h * DH:(h + 1) * DH],
                                       rmt[:, j:j + 1], zutile[:, :],
                                       ALU.mult, ALU.add)
                sV.inc(e.drain())
            V(_mask)
            def _trp(e):
                sV.wait(e)
                e.transpose(pb[0][:, 0:128], putile[:, :], ident[:, :])
                sT.inc(e.drain())
            T(_trp)
            def _evp(e):
                sT.wait(e)
                sV.inc(e.tensor_copy(putt[:, :], pb[0][:, 0:128]))
            V(_evp)
            def _fold(e, h=h):
                sV.wait(e)
                e.matmul(pb[1][:, 0:128], putt[:, :], s_in[:, h * DH:(h + 1) * DH],
                         start=True, stop=True)
                sT.inc(e.drain())
            T(_fold)
            def _fv(e, j=j, h=h):
                sT.wait(e)
                e.scalar_tensor_tensor(s_in[:, h * DH:(h + 1) * DH],
                                       pzbuf[:, (H + h) * DH:(H + h + 1) * DH],
                                       rmt[:, j:j + 1], pb[1][:, 0:128],
                                       ALU.mult, ALU.add)
                sV.inc(e.drain())
            V(_fv)

    olbuf = A.alloc(H * DH, parts=64)
    ggt = A.alloc(H * DH, parts=64)
    ggld = A.alloc(KF * CT)
    qcbuf = A.alloc(CT)
    A.alloc(8)
    msum = A.sub(8, 64, 0)
    osq = A.alloc(H * DH, parts=64)
    ofeat = A.alloc(KF * LTOK)
    woslab = A.alloc(2 * 1024)
    assert A.off <= ART, A.off
    ofe = ofeat.rearrange("p (f t) -> p f t", f=KF)

    if DBG3:
        def _d3(e):
            sV.wait(e)
            sT.wait(e)
            sD.inc(e.dma_start(out=dbg[:, 0:1024], in_=s_in[:, :]), 16)
            sD.wait(e)
        SY(_d3)

    for cg in range(NCHUNK):
        ts = cg * CT
        def _old(e, cg=cg):
            sV.wait(e)
            sT.wait(e)
            for h in range(H):
                bi = cg * H + h
                sD.inc(e.dma_start(out=olbuf[0:64, h * DH:(h + 1) * DH],
                                   in_=dOl[bi * CT:(bi + 1) * CT, :]), 16)
        SY(_old)
        for h in range(H):
            def _qld(e, cg=cg, h=h):
                sT.wait(e)
                bi = cg * H + h
                sD.inc(e.dma_start(out=qcbuf[:, :],
                                   in_=dQc[:, bi * CT:(bi + 1) * CT]), 16)
            SY(_qld)
            def _om(e, h=h):
                sD.wait(e)
                b = pb[0] if h < 4 else pb[1]
                e.matmul(b[0:64, (h % 4) * 128:(h % 4) * 128 + 128], qcbuf[:, :],
                         s_in[:, h * DH:(h + 1) * DH], start=True, stop=True)
                sT.inc(e.drain())
            T(_om)
        def _oadd(e):
            sT.wait(e)
            sD.wait(e)
            e.tensor_add(olbuf[0:64, 0:512], olbuf[0:64, 0:512], pb[0][0:64, 0:512])
            e.tensor_add(olbuf[0:64, 512:1024], olbuf[0:64, 512:1024],
                         pb[1][0:64, 0:512])
            sV.inc(e.drain())
        V(_oadd)
        def _ggl(e, ts=ts):
            sT.wait(e)
            gl = ggld.rearrange("p (f t) -> p f t", f=KF)
            for fc in range(KF):
                sD.inc(e.dma_start(out=gl[:, fc, :],
                                   in_=dgg[fc * 128:(fc + 1) * 128, ts:ts + CT]), 16)
        SY(_ggl)
        for h in range(H):
            def _ggt(e, h=h, ts=ts):
                if h == 0:
                    sS.wait(e)
                    sD.wait(e)
                b = pb[2] if h < 4 else pb[3]
                gl = ggld.rearrange("p (f t) -> p f t", f=KF)
                e.transpose(b[0:64, (h % 4) * 128:(h % 4) * 128 + 128],
                            gl[:, h, :], ident[:, :])
                sT.inc(e.drain())
            T(_ggt)
        def _ggs(e):
            sT.wait(e)
            e.activation(ggt[0:64, 0:512], pb[2][0:64, 0:512], AF.Silu)
            e.activation(ggt[0:64, 512:1024], pb[3][0:64, 0:512], AF.Silu)
            sS.inc(e.drain())
        S(_ggs)
        def _rms1(e):
            sV.wait(e)
            sS.wait(e)
            h4 = H // 2
            for half in (0, 1):
                sl = slice(half * 512, half * 512 + 512)
                e.tensor_tensor(osq[0:64, sl].rearrange("p (h d) -> p h d", h=h4),
                                olbuf[0:64, sl].rearrange("p (h d) -> p h d", h=h4),
                                olbuf[0:64, sl].rearrange("p (h d) -> p h d", h=h4),
                                ALU.mult)
            for half in (0, 1):
                sl = slice(half * 512, half * 512 + 512)
                e.tensor_reduce(msum[0:64, half * h4:half * h4 + h4],
                                osq[0:64, sl].rearrange("p (h d) -> p h d", h=h4),
                                AX.X, ALU.add)
            sV.inc(e.drain())
        V(_rms1)
        def _rms2(e):
            sV.wait(e)
            e.activation(msum[0:64, :], msum[0:64, :], AF.Sqrt,
                         scale=rdhcol[0:64, 0:1], bias=epscol[0:64, 0:1])
            sS.inc(e.drain())
        S(_rms2)
        def _rms3(e):
            sS.wait(e)
            ol3 = olbuf[0:64, :].rearrange("p (h d) -> p h d", h=H)
            e.reciprocal(msum[0:64, :], msum[0:64, :])
            e.tensor_tensor(ol3, ol3,
                            onwrep[0:64, :].rearrange("p (o d) -> p o d", o=1)
                            .broadcast_to([64, H, DH]), ALU.mult)
            e.tensor_tensor(ol3, ol3,
                            msum[0:64, :].rearrange("p (h o) -> p h o", o=1)
                            .broadcast_to([64, H, DH]), ALU.mult)
            e.tensor_mul(olbuf[0:64, :], olbuf[0:64, :], ggt[0:64, :])
            sV.inc(e.drain())
        V(_rms3)
        if DBG5 == cg:
            def _d5(e, cg=cg):
                sV.wait(e)
                sS.wait(e)
                sD.inc(e.dma_start(out=dbg[0:64, 0:1024], in_=olbuf[0:64, :]), 16)
                sD.inc(e.dma_start(out=dbg[0:64, 1024:2048], in_=ggt[0:64, :]), 16)
                sD.inc(e.dma_start(out=dbg[0:64, 2048:2056], in_=msum[0:64, :]), 16)
                sD.inc(e.dma_start(out=dbg[0:64, 2056:3080], in_=osq[0:64, :]), 16)
                sD.inc(e.dma_start(out=dbg[0:64, 3080:3208],
                                   in_=onwrep[0:64, :]), 16)
                sD.inc(e.dma_start(out=dbg[:, 3208:3272], in_=ggld[:, 0:64]), 16)
                sD.wait(e)
            SY(_d5)
            ctx.close()
            return nc
        for h in range(H):
            def _oft(e, h=h):
                if h == 0:
                    sV.wait(e)
                e.transpose(pb[4][:, h * 64:(h + 1) * 64],
                            olbuf[0:64, h * DH:(h + 1) * DH], ident[0:64, 0:64])
                sT.inc(e.drain())
            T(_oft)
        def _ofv(e, ts=ts):
            sT.wait(e)
            for h in range(H):
                e.tensor_copy(ofe[:, h, ts:ts + CT], pb[4][:, h * 64:(h + 1) * 64])
            sV.inc(e.drain())
        V(_ofv)

    if DBG6:
        def _d6(e):
            sV.wait(e)
            sD.inc(e.dma_start(out=dbg[:, 0:4096], in_=ofeat[:, 0:4096]), 16)
            sD.inc(e.dma_start(out=dbg[:, 4096:4112], in_=gam[:, :]), 16)
            sD.inc(e.dma_start(out=dbg[:, 4112:4368],
                               in_=xres[:, 3:259]), 16)
            sD.wait(e)
        SY(_d6)
        ctx.close()
        return nc

    for dc in range(NDC):
        def _wod(e, dc=dc):
            half = (dc % 2) * 1024
            if dc >= 2:
                e.wait_ge(sT.h, sT.n - 2)
            else:
                sT.wait(e)
            sW.inc(e.dma_start(
                out=woslab[:, half:half + 1024].rearrange("p (f d) -> p f d", f=8),
                in_=woT[:, dc * 128:(dc + 1) * 128]
                .rearrange("(f p) d -> p f d", p=128)), 16)
        G(_wod)
        for tt in range(2):
            def _wom(e, dc=dc, tt=tt):
                if tt == 0:
                    sV.wait(e)
                half = (dc % 2) * 1024
                sW.wait(e)
                for fk in range(8):
                    e.matmul(pb[5 + tt][:, 0:512],
                             woslab[:, half + fk * 128:half + (fk + 1) * 128],
                             ofe[:, fk, tt * 512:(tt + 1) * 512],
                             start=(fk == 0), stop=(fk == 7))
                sT.inc(e.drain())
            T(_wom)
        def _wov(e, dc=dc):
            sT.wait(e)
            for tt in range(2):
                xv = xre[:, dc, 3 + tt * 512:3 + (tt + 1) * 512]
                e.scalar_tensor_tensor(xv, pb[5 + tt][:, 0:512],
                                       gam[:, dc:dc + 1], xv, ALU.mult, ALU.add)
            sV.inc(e.drain())
        V(_wov)
        def _wos(e, dc=dc):
            sV.wait(e)
            sD.inc(e.dma_start(out=dx1[dc * 128:(dc + 1) * 128, :],
                               in_=xre[:, dc, 3:3 + LTOK]), 16)
        SY(_wos)

    barrier()

    A.reset(P2M)
    x1s = A.alloc(FTT)
    x1s2 = A.alloc(FTT)
    h2s = A.alloc(FTT)
    h2b_r = A.alloc(NDC * FTT)
    fuse_r = A.alloc(48 * FTT // 2 * 2)
    ftmp = A.alloc(FTT)
    fslab = A.alloc(2 * 2048)
    b8_r = A.alloc(2 * 1024)
    A.alloc(LTOK)
    lnr2 = A.sub(LTOK, 1, 0)[:, 0:2 * FTT]
    xsq2 = A.alloc(FTT)
    assert A.off <= ART, A.off
    q8t = ctx.enter_context(nc.sbuf_tensor("q8t", [128, 2048], FP8))
    h2b = h2b_r.bitcast(BF16)
    h2hi = h2b[:, 0:NDC * FTT].rearrange("p (d t) -> p d t", d=NDC)
    h2lo = h2b[:, NDC * FTT:2 * NDC * FTT].rearrange("p (d t) -> p d t", d=NDC)
    fuseb = fuse_r.bitcast(BF16)
    FW = 48 * FTT
    fhi = fuseb[:, 0:FW].rearrange("p (f t) -> p f t", f=48)
    flo = fuseb[:, FW:2 * FW].rearrange("p (f t) -> p f t", f=48)
    b8 = b8_r.bitcast(BF16)

    for ft in range(NFTT):
        tt0 = ft * FTT
        for dc in range(NDC):
            def _xl(e, dc=dc, ft=ft):
                sV.wait(e)
                sS.wait(e)
                sD.inc(e.dma_start(out=x1s[:, :],
                                   in_=dx1[dc * 128:(dc + 1) * 128,
                                           tt0:tt0 + FTT]), 16)
            SY(_xl)
            def _sq(e, dc=dc):
                sD.wait(e)
                if dc > 0:
                    sT.wait(e)
                e.activation(xsq2[:, :], x1s[:, :], AF.Square)
                sS.inc(e.drain())
            S(_sq)
            def _mm(e, dc=dc):
                sS.wait(e)
                e.matmul(pb[0][0:1, 0:FTT], onescol[:, 0:1], x1s[:, :],
                         start=(dc == 0), stop=(dc == NDC - 1))
                e.matmul(pb[1][0:1, 0:FTT], onescol[:, 0:1], xsq2[:, :],
                         start=(dc == 0), stop=(dc == NDC - 1))
                sT.inc(e.drain())
            T(_mm)
            def _w8(e, dc=dc):
                sT.wait(e)
                sV.inc(e.drain())
            V(_w8)
        def _f_v(e):
            sT.wait(e)
            e.tensor_scalar_mul(lnr2[0:1, 0:FTT], pb[0][0:1, 0:FTT], 1.0 / HID)
            e.tensor_scalar_mul(lnr2[0:1, FTT:2 * FTT], pb[1][0:1, 0:FTT],
                                1.0 / HID)
            e.tensor_mul(xsq2[0:1, 0:FTT], lnr2[0:1, 0:FTT], lnr2[0:1, 0:FTT])
            e.tensor_sub(lnr2[0:1, FTT:2 * FTT], lnr2[0:1, FTT:2 * FTT],
                         xsq2[0:1, 0:FTT])
            sV.inc(e.drain())
        V(_f_v)
        def _f_s(e):
            sV.wait(e)
            e.activation(lnr2[0:1, FTT:2 * FTT], lnr2[0:1, FTT:2 * FTT],
                         AF.Sqrt, bias=epscol[0:1, 0:1])
            sS.inc(e.drain())
        S(_f_s)
        def _f_v2(e):
            sS.wait(e)
            sV.inc(e.reciprocal(lnr2[0:1, FTT:2 * FTT], lnr2[0:1, FTT:2 * FTT]))
        V(_f_v2)
        def _f_rep(e):
            sV.wait(e)
            e.matmul(pb[2][:, 0:FTT], onesr[0:1, 0:128], lnr2[0:1, 0:FTT],
                     start=True, stop=True)
            e.matmul(pb[3][:, 0:FTT], onesr[0:1, 0:128], lnr2[0:1, FTT:2 * FTT],
                     start=True, stop=True)
            sT.inc(e.drain())
        T(_f_rep)
        for dc in range(NDC):
            def _xl2(e, dc=dc, ft=ft):
                if dc == 0:
                    sT.wait(e)
                sV.wait(e)
                sD.inc(e.dma_start(out=x1s[:, :],
                                   in_=dx1[dc * 128:(dc + 1) * 128,
                                           tt0:tt0 + FTT]), 16)
            SY(_xl2)
            def _h2(e, dc=dc):
                sD.wait(e)
                e.tensor_sub(h2s[:, :], x1s[:, :], pb[2][:, 0:FTT])
                e.tensor_mul(h2s[:, :], h2s[:, :], pb[3][:, 0:FTT])
                e.tensor_scalar(h2s[:, :], h2s[:, :],
                                onecbl[:, dc:dc + 1], modfm[:, 48 + dc:49 + dc],
                                ALU.mult, ALU.add)
                e.tensor_copy(h2hi[:, dc, :], h2s[:, :])
                e.tensor_sub(h2lo[:, dc, :], h2s[:, :], h2hi[:, dc, :])
                sV.inc(e.drain())
            V(_h2)

        def emit_gmm(dc, fb, ft):
            def _gmm(e, dc=dc, fb=fb, ft=ft):
                if ft == 0:
                    sS.wait(e)
                else:
                    sW.wait(e)
                if dc == 0:
                    sV.wait(e)
                bh = (dc % 2) * 2048 if ft == 0 else (dc % 4) * 1024
                for ps, h2x in ((0, h2hi), (1, h2lo)):
                    for fc in range(4):
                        e.matmul(pb[fc][:, 0:FTT],
                                 b8[:, bh + fc * 128:bh + (fc + 1) * 128],
                                 h2x[:, dc, :],
                                 start=(dc == 0 and ps == 0),
                                 stop=(dc == NDC - 1 and ps == 1))
                        e.matmul(pb[4 + fc][:, 0:FTT],
                                 b8[:, bh + 512 + fc * 128:
                                    bh + 512 + (fc + 1) * 128],
                                 h2x[:, dc, :],
                                 start=(dc == 0 and ps == 0),
                                 stop=(dc == NDC - 1 and ps == 1))
                sT.inc(e.drain())
            T(_gmm)

        for fb in range(12):
            if ft == 0:
                for dc in range(NDC):
                    def _wd1(e, dc=dc, fb=fb):
                        half = (dc % 2) * 1024
                        if dc >= 2:
                            e.wait_ge(sT.h, sT.n - 1)
                        else:
                            sT.wait(e)
                        for wi, wsrc in ((0, wgate), (1, wup)):
                            sW.inc(e.dma_start(
                                out=fslab[:, wi * 512 + half:wi * 512 + half + 512],
                                in_=wsrc[dc * 128:(dc + 1) * 128,
                                         fb * 512:(fb + 1) * 512]), 16)
                    G(_wd1)
                    def _cv(e, dc=dc, fb=fb):
                        sS.wait(e)
                        half = (dc % 2) * 1024
                        sW.wait(e)
                        e.tensor_copy(q8t[:, 0:512], fslab[:, half:half + 512])
                        e.tensor_copy(q8t[:, 512:1024],
                                      fslab[:, 512 + half:1024 + half])
                        sV.inc(e.drain())
                    V(_cv)
                    def _cs(e, dc=dc):
                        sV.wait(e)
                        if dc >= 2:
                            e.wait_ge(sT.h, sT.n - 1)
                            e.wait_ge(sD.h, sD.n - 32)
                        else:
                            sD.wait(e)
                            sT.wait(e)
                        bh = (dc % 2) * 2048
                        e.copy(b8[:, bh:bh + 1024], q8t[:, 0:1024])
                        sS.inc(e.drain())
                    S(_cs)
                    def _w8(e, dc=dc, fb=fb):
                        sS.wait(e)
                        bh = (dc % 2) * 2048
                        sD.inc(e.dma_start(
                            out=d8g[dc * 128:(dc + 1) * 128, fb * 512:(fb + 1) * 512],
                            in_=b8[:, bh:bh + 512]), 16)
                        sD.inc(e.dma_start(
                            out=d8u[dc * 128:(dc + 1) * 128, fb * 512:(fb + 1) * 512],
                            in_=b8[:, bh + 512:bh + 1024]), 16)
                    SY(_w8)
                    emit_gmm(dc, fb, ft)
            else:
                for dc in range(NDC):
                    def _wd8(e, dc=dc, fb=fb):
                        bh = (dc % 4) * 1024
                        e.wait_ge(sT.h, sT.n - 3)
                        sW.inc(e.dma_start(
                            out=b8[:, bh:bh + 512],
                            in_=d8g[dc * 128:(dc + 1) * 128,
                                    fb * 512:(fb + 1) * 512]), 16)
                        sW.inc(e.dma_start(
                            out=b8[:, bh + 512:bh + 1024],
                            in_=d8u[dc * 128:(dc + 1) * 128,
                                    fb * 512:(fb + 1) * 512]), 16)
                    G(_wd8)
                    emit_gmm(dc, fb, ft)
            if False:
                def _gmm(e, dc=dc, fb=fb, ft=ft):
                    if ft == 0:
                        sS.wait(e)
                    else:
                        sW.wait(e)
                    if dc == 0:
                        sV.wait(e)
                    bh = (dc % 2) * 2048
                    for ps, h2x in ((0, h2hi), (1, h2lo)):
                        for fc in range(4):
                            e.matmul(pb[fc][:, 0:FTT],
                                     b8[:, bh + fc * 128:bh + (fc + 1) * 128],
                                     h2x[:, dc, :],
                                     start=(dc == 0 and ps == 0),
                                     stop=(dc == NDC - 1 and ps == 1))
                            e.matmul(pb[4 + fc][:, 0:FTT],
                                     b8[:, bh + 512 + fc * 128:
                                        bh + 512 + (fc + 1) * 128],
                                     h2x[:, dc, :],
                                     start=(dc == 0 and ps == 0),
                                     stop=(dc == NDC - 1 and ps == 1))
                    sT.inc(e.drain())
                T(_gmm)
            if DBG8 and ft == 0 and fb == 0:
                def _d8(e):
                    sT.wait(e)
                    e.tensor_copy(xsq[:, 0:FTT] if False else dbg[:, 0:0],
                                  dbg[:, 0:0]) if False else None
                    sV.inc(e.drain())
                V(_d8)
                def _d8v(e):
                    sT.wait(e)
                    e.tensor_copy(x1s[:, :], pb[0][:, 0:FTT])
                    e.tensor_copy(x1s2[:, :], pb[4][:, 0:FTT])
                    sV.inc(e.drain())
                V(_d8v)
                def _d8d(e):
                    sV.wait(e)
                    sD.inc(e.dma_start(out=dbg[:, 0:256], in_=x1s[:, :]), 16)
                    sD.inc(e.dma_start(out=dbg[:, 256:512], in_=x1s2[:, :]), 16)
                    sD.inc(e.dma_start(out=dbg[:, 512:1536], in_=h2b_r[:, 0:1024]), 16)
                    sD.inc(e.dma_start(out=dbg[:, 1536:2048], in_=b8_r[:, 0:512]), 16)
                    sD.wait(e)
                SY(_d8d)
                ctx.close()
                return nc
            for fc in range(4):
                def _ge(e, fb=fb, fc=fc):
                    sT.wait(e)
                    sV.wait(e)
                    gcol = fb * 4 + fc
                    e.activation(xsq2[:, 0:FTT], pb[fc][:, 0:FTT], AF.Silu,
                                 scale=gscf[:, gcol:gcol + 1])
                    sS.inc(e.drain())
                S(_ge)
                def _ue(e, fb=fb, fc=fc):
                    sS.wait(e)
                    gcol = fb * 4 + fc
                    e.scalar_tensor_tensor(ftmp[:, :], pb[4 + fc][:, 0:FTT],
                                           uscf[:, gcol:gcol + 1], xsq2[:, 0:FTT],
                                           ALU.mult, ALU.mult)
                    e.tensor_copy(fhi[:, gcol, :], ftmp[:, :])
                    e.tensor_sub(flo[:, gcol, :], ftmp[:, :], fhi[:, gcol, :])
                    sV.inc(e.drain())
                V(_ue)

        if DBG7 and ft == 0:
            def _d7(e):
                sV.wait(e)
                sS.wait(e)
                sD.inc(e.dma_start(out=dbg[:, 0:1024], in_=h2b_r[:, 0:1024]), 16)
                sD.inc(e.dma_start(out=dbg[:, 1024:2048], in_=fuse_r[:, 0:1024]), 16)
                sD.inc(e.dma_start(out=dbg[:, 2048:3072],
                                   in_=fuse_r[:, FW // 2:FW // 2 + 1024]), 16)
                sD.inc(e.dma_start(out=dbg[:, 3072:3584], in_=b8_r[:, 0:512]), 16)
                sD.inc(e.dma_start(out=dbg[:, 4096:4352], in_=xsq2[:, 0:FTT]), 16)
                sD.inc(e.dma_start(out=dbg[:, 4352:5376],
                                   in_=h2b_r[:, 2048:3072]), 16)
                sD.wait(e)
            SY(_d7)
            ctx.close()
            return nc
        for kc in range(48):
            if ft == 0:
                def _dwd(e, kc=kc):
                    half = (kc % 2) * 2048
                    if kc >= 2:
                        e.wait_ge(sT.h, sT.n - 1)
                    else:
                        sT.wait(e)
                    sW.inc(e.dma_start(out=fslab[:, half:half + 2048],
                                       in_=wdown[kc * 128:(kc + 1) * 128, :]), 16)
                G(_dwd)
                def _dcv(e, kc=kc):
                    sS.wait(e)
                    sW.wait(e)
                    half = (kc % 2) * 2048
                    e.tensor_copy(q8t[:, 0:2048], fslab[:, half:half + 2048])
                    sV.inc(e.drain())
                V(_dcv)
                def _dcs(e, kc=kc):
                    sV.wait(e)
                    if kc >= 2:
                        e.wait_ge(sT.h, sT.n - 1)
                        e.wait_ge(sD.h, sD.n - 16)
                    else:
                        sD.wait(e)
                        sT.wait(e)
                    bh = (kc % 2) * 2048
                    e.copy(b8[:, bh:bh + 2048], q8t[:, 0:2048])
                    sS.inc(e.drain())
                S(_dcs)
                def _dw8(e, kc=kc):
                    sS.wait(e)
                    bh = (kc % 2) * 2048
                    sD.inc(e.dma_start(out=d8d[kc * 128:(kc + 1) * 128, :],
                                       in_=b8[:, bh:bh + 2048]), 16)
                SY(_dw8)
            else:
                def _dwd8(e, kc=kc):
                    bh = (kc % 2) * 2048
                    if kc >= 2:
                        e.wait_ge(sT.h, sT.n - 1)
                    else:
                        sT.wait(e)
                    sW.inc(e.dma_start(out=b8[:, bh:bh + 2048],
                                       in_=d8d[kc * 128:(kc + 1) * 128, :]), 16)
                G(_dwd8)
            def _dmm(e, kc=kc, ft=ft):
                if ft == 0:
                    sS.wait(e)
                else:
                    sW.wait(e)
                if kc == 0:
                    sV.wait(e)
                bh = (kc % 2) * 2048
                for ps, fx in ((0, fhi), (1, flo)):
                    for dco in range(NDC):
                        e.matmul(pb[dco // 2][:, (dco % 2) * 256:(dco % 2) * 256 + FTT],
                                 b8[:, bh + dco * 128:bh + (dco + 1) * 128],
                                 fx[:, kc, :],
                                 start=(kc == 0 and dco % 2 == 0 and ps == 0),
                                 stop=(kc == 47 and dco % 2 == 1 and ps == 1))
                sT.inc(e.drain())
            T(_dmm)
        for dco in range(NDC):
            def _xl3(e, dco=dco, ft=ft):
                if dco == 0:
                    sT.wait(e)
                sV.wait(e)
                sD.wait(e)
                sD.inc(e.dma_start(out=x1s2[:, :],
                                   in_=dx1[dco * 128:(dco + 1) * 128,
                                           tt0:tt0 + FTT]), 16)
            SY(_xl3)
            def _dev(e, dco=dco):
                sD.wait(e)
                e.scalar_tensor_tensor(
                    x1s2[:, :],
                    pb[dco // 2][:, (dco % 2) * 256:(dco % 2) * 256 + FTT],
                    dsgb[:, dco:dco + 1], x1s2[:, :], ALU.mult, ALU.add)
                sV.inc(e.drain())
            V(_dev)
            def _ost(e, dco=dco, ft=ft):
                sV.wait(e)
                sD.inc(e.dma_start(out=out[dco * 128:(dco + 1) * 128,
                                           tt0:tt0 + FTT],
                                   in_=x1s2[:, :]), 16)
            SY(_ost)

    def _fin(e):
        sD.wait(e)
        sW.wait(e)
    SY(_fin)

    ctx.close()
    return nc


_NC_CACHE = None


def kernel(**inputs):
    global _NC_CACHE
    x = np.asarray(inputs["L0_attn_x"], np.float32)[0]
    Wq = np.asarray(inputs["L0_attn_Wq"], np.float32)
    Wk = np.asarray(inputs["L0_attn_Wk"], np.float32)
    Wv = np.asarray(inputs["L0_attn_Wv"], np.float32)
    Wb = np.asarray(inputs["L0_attn_Wb"], np.float32)
    Wa = np.asarray(inputs["L0_attn_Wa"], np.float32)
    Wg = np.asarray(inputs["L0_attn_Wg"], np.float32)
    Wo = np.asarray(inputs["L0_attn_Wo"], np.float32)

    wqT = np.ascontiguousarray(Wq.T)
    wkT = np.ascontiguousarray(Wk.T)
    wvT = np.ascontiguousarray(Wv.T)
    wbaT = np.ascontiguousarray(np.concatenate([Wb, Wa], 0).T)
    wgT = np.ascontiguousarray(Wg.T)
    woT = np.ascontiguousarray(Wo.T)
    wgate = np.ascontiguousarray(np.asarray(inputs["L0_ffn_gate_w"], np.float32))
    wup = np.ascontiguousarray(np.asarray(inputs["L0_ffn_up_w"], np.float32))
    wdown = np.ascontiguousarray(np.asarray(inputs["L0_ffn_down_w"], np.float32))
    wmod = np.asarray(inputs["L0_wmod"], np.float32)

    if _NC_CACHE is None:
        _NC_CACHE = build_nc()
    nc = _NC_CACHE

    in_maps = []
    for core in range(NCORES):
        lo = core * LTOK
        if core == 0:
            halo = np.zeros((3, HID), np.float32)
        else:
            halo = x[lo - 3:lo]
        xs = np.concatenate([halo, x[lo:lo + LTOK]], 0)
        m = {
            "xT": np.ascontiguousarray(xs.T),
            "wqT": wqT, "wkT": wkT, "wvT": wvT, "wbaT": wbaT, "wgT": wgT,
            "woT": woT, "wgate": wgate, "wup": wup, "wdown": wdown,
            "convq": np.ascontiguousarray(np.asarray(inputs["L0_attn_conv_q"], np.float32)),
            "convk": np.ascontiguousarray(np.asarray(inputs["L0_attn_conv_k"], np.float32)),
            "convv": np.ascontiguousarray(np.asarray(inputs["L0_attn_conv_v"], np.float32)),
            "alog": np.asarray(inputs["L0_attn_A_log"], np.float32).reshape(1, 8),
            "dtb": np.asarray(inputs["L0_attn_dt_bias"], np.float32).reshape(1, 8),
            "onw": np.asarray(inputs["L0_attn_o_norm_w"], np.float32).reshape(1, 128),
            "gsc": np.asarray(inputs["L0_ffn_gate_scale"], np.float32).reshape(1, -1),
            "usc": np.asarray(inputs["L0_ffn_up_scale"], np.float32).reshape(1, -1),
            "dsc": np.asarray(inputs["L0_ffn_down_scale"], np.float32).reshape(1, -1),
            "cvec": np.asarray(inputs["L0_c"], np.float32).reshape(1, -1),
            "wmodsl": np.ascontiguousarray(
                wmod[:, core * (12288 // NCORES):(core + 1) * (12288 // NCORES)]),
            "hz": np.full((128, 3), 0.0 if core == 0 else 1.0, np.float32),
            "rmask": np.ascontiguousarray(
                np.tile((np.arange(8) < core).astype(np.float32), (128, 1))),
            "rmaskinv": np.ascontiguousarray(
                np.tile((np.arange(8) >= core).astype(np.float32), (128, 1))),
        }
        in_maps.append(m)

    res = run_bass_kernel_spmd(nc, in_maps, core_ids=list(range(NCORES)))
    parts = [np.asarray(r["out"]).T for r in res.results]
    return np.concatenate(parts, 0)[None]
